# version 08
# baseline (speedup 1.0000x reference)
import numpy as np
import ml_dtypes
import concourse.bass as bass
import concourse.mybir as mybir
from concourse.bass_utils import run_bass_kernel_spmd

F32 = mybir.dt.float32
I32 = mybir.dt.int32
BF16 = mybir.dt.bfloat16
AF = mybir.ActivationFunctionType
ALU = mybir.AluOpType
AX = mybir.AxisListType

S = 4096
D = 1024
DH = 512
HID = 4096
NT = 32
NOWN = 16
NCH = 16
ATT_SC = float(DH) ** -0.5
CSH = 15.0
LN2 = float(np.log(2.0))
TINY = 1e-37
EPS = 1e-5


class Prog:
    def __init__(self):
        self.ops = []
        self.cnt = {}

    def v(self, s):
        return self.cnt.get(s, 0)

    def op(self, eng, fn, wait=(), inc=None):
        if inc is not None:
            s, n = inc
            self.cnt[s] = self.cnt.get(s, 0) + n
        self.ops.append((eng, fn, tuple(wait), inc))

    def dma(self, eng, sem, out, in_, wait=(), **kw):
        self.op(eng, lambda e, o=out, i=in_, k=kw: e.dma_start(out=o, in_=i, **k),
                wait=wait, inc=(sem, 16))


def run_stream(prog, name, eng, sems):
    for e, fn, waits, inc in prog.ops:
        if e != name:
            continue
        for s, v in waits:
            if v > 0:
                eng.wait_ge(sems[s], v)
        ins = fn(eng)
        if inc is not None:
            ins.then_inc(sems[inc[0]], inc[1])


def build(dbg=False):
    nc = bass.Bass("TRN2", target_bir_lowering=False)

    xf_d = nc.dram_tensor("x_full", [S, D], F32, kind="ExternalInput").ap()
    xo_d = nc.dram_tensor("x_own", [NOWN * 128, D], F32, kind="ExternalInput").ap()
    wqkv_d = nc.dram_tensor("wqkv", [1536, D], F32, kind="ExternalInput").ap()
    wout_d = nc.dram_tensor("wout", [D, DH], F32, kind="ExternalInput").ap()
    bout_d = nc.dram_tensor("bout", [1, D], F32, kind="ExternalInput").ap()
    wfc_d = nc.dram_tensor("wfc", [HID, D], F32, kind="ExternalInput").ap()
    bfc_d = nc.dram_tensor("bfc", [1, HID], F32, kind="ExternalInput").ap()
    wpj_d = nc.dram_tensor("wproj", [D, HID], F32, kind="ExternalInput").ap()
    bpj_d = nc.dram_tensor("bproj", [1, D], F32, kind="ExternalInput").ap()
    m0_d = nc.dram_tensor("m0", [128, 128], BF16, kind="ExternalInput").ap()
    m1_d = nc.dram_tensor("m1", [128, 128], BF16, kind="ExternalInput").ap()
    onc_d = nc.dram_tensor("ones_col", [128, 1], BF16, kind="ExternalInput").ap()
    onr_d = nc.dram_tensor("ones_row", [1, 128], F32, kind="ExternalInput").ap()
    o11_d = nc.dram_tensor("one11", [1, 1], F32, kind="ExternalInput").ap()
    kon_d = nc.dram_tensor("konst", [128, 8], F32, kind="ExternalInput").ap()
    y_d = nc.dram_tensor("y", [NOWN * 128, D], F32, kind="ExternalOutput").ap()
    SCR = "ExternalOutput" if dbg else "Internal"
    wfcq_s = nc.dram_tensor("wfcq_s", [HID, D], BF16, kind=SCR).ap()
    wpqT_s = nc.dram_tensor("wpqT_s", [HID, D], BF16, kind=SCR).ap()
    xq_s = nc.dram_tensor("xq_s", [128, 8 * 2048], BF16, kind=SCR).ap()
    if dbg:
        dbg_kfm = nc.dram_tensor("dbg_kfm", [128, 4 * S], BF16, kind="ExternalOutput").ap()
        dbg_qfm = nc.dram_tensor("dbg_qfm", [128, 4 * 2048], BF16, kind="ExternalOutput").ap()
        dbg_vtm = nc.dram_tensor("dbg_vtm", [128, NT * DH], BF16, kind="ExternalOutput").ap()
        dbg_acc = nc.dram_tensor("dbg_acc", [128, NOWN * D], BF16, kind="ExternalOutput").ap()
        dbg_rec = nc.dram_tensor("dbg_rec", [128, NOWN], F32, kind="ExternalOutput").ap()
        dbg_stats = nc.dram_tensor("dbg_stats", [128, 4 * NT], F32, kind="ExternalOutput").ap()
        dbg_h32 = nc.dram_tensor("dbg_h32", [128, 32 * 512], F32, kind="ExternalOutput").ap()
        dbg_hq = nc.dram_tensor("dbg_hq", [128, 32 * 512], BF16, kind="ExternalOutput").ap()

    SZ = {F32: 4, I32: 4, BF16: 2}
    ptr = [0]

    BASE0 = 16576

    def bump(name, shape, dtype, align=64):
        pp = (ptr[0] + align - 1) // align * align
        nb = int(np.prod(shape[1:])) * SZ[dtype] if len(shape) > 1 else SZ[dtype]
        ptr[0] = pp + nb
        return nc.alloc_sbuf_tensor_at(name, list(shape), dtype, offset=BASE0 + pp).ap()

    ssum = bump("ssum", [128, NT], F32)
    ssq = bump("ssq", [128, NT], F32)
    negm = bump("negm", [128, NT], F32)
    rstd = bump("rstd", [128, NT], F32)
    st1 = bump("st1", [128, NT], F32)
    st2 = bump("st2", [128, NT], F32)
    c_am = bump("c_am", [128, 512], F32)
    c_t = bump("c_t", [128, 512], F32)
    c_i = bump("c_i", [128, 512], I32)
    c_f = bump("c_f", [128, 512], F32)
    c_s = bump("c_s", [128, 512], F32)
    c_si = bump("c_si", [128, 512], F32)
    c_bf = bump("c_bf", [128, 512], BF16)
    m0_t = bump("m0_t", [128, 128], BF16)
    m1_t = bump("m1_t", [128, 128], BF16)
    onc = bump("onc", [128, 1], BF16)
    onr = bump("onr", [1, 128], F32)
    o11 = bump("o11", [1, 1], F32)
    kon = bump("kon", [128, 8], F32)
    K_TINY, K_EPS, K_NCSH, K_LN2, K_NLN2, K_NHALF, K_ASC = (
        kon[:, 0:1], kon[:, 1:2], kon[:, 2:3], kon[:, 3:4], kon[:, 4:5],
        kon[:, 5:6], kon[:, 6:7])
    bsum_bc = bump("bsum_bc", [128, D], F32)
    bfc_cols = bump("bfc_cols", [128, 32], F32)
    recips = bump("recips", [128, NOWN], F32)
    denom_r = bump("denom_r", [1, 512], F32)
    xpb = bump("xpb", [128, D], F32)
    sdum = bump("sdum", [128, 1], F32)
    acc = bump("acc", [128, NOWN * D], BF16)
    PERSIST_END = ptr[0]

    wqkvT = bump("wqkvT", [128, 8 * 1536], BF16)
    woutT = bump("woutT", [128, 4 * D], BF16)
    STAGE0 = ptr[0]
    kfm = bump("kfm", [128, 4 * S], BF16)
    qfm = bump("qfm", [128, 4 * 2048], BF16)
    vtm = bump("vtm", [128, NT * DH], BF16)
    P01_BASE = ptr[0]
    KB = 1024

    def at_off(name, off_kb, shape, dtype):
        return nc.alloc_sbuf_tensor_at(name, list(shape), dtype,
                                       offset=BASE0 + P01_BASE + int(off_kb * KB)).ap()

    xstream = at_off("xstream", 0, [128, 2 * D], F32)
    sqdump = at_off("sqdump", 8, [128, D], F32)
    xnbf = at_off("xnbf", 8, [128, 2 * D], BF16)
    xnfm = at_off("xnfm", 12, [128, 8 * 4 * 128], BF16)
    brow2 = at_off("brow2", 12, [1, 2 * D], F32)
    xcload = at_off("xcload", 0, [128, D], F32)
    xn32 = at_off("xn32", 4, [128, D], F32)
    xqbf = at_off("xqbf", 8, [128, D], BF16)
    xqst = at_off("xqst", 10, [128, 2 * 8 * 128], BF16)
    expP = at_off("expP", 14, [128, 2 * 512], BF16)
    avun = at_off("avun", 16, [128, 4 * 512], BF16)
    wraw = at_off("wraw", 20, [128, 2 * 1024], F32)
    wqbf = at_off("wqbf", 28, [128, 2 * 1024], BF16)
    wtst = at_off("wtst", 32, [128, 1024], BF16)
    vfmb = at_off("vfmb", 34, [128, 4 * 256], BF16)
    A_END = ptr[0] + 36 * KB
    ptr[0] = A_END
    P2_END = A_END

    ptr[0] = PERSIST_END
    h32 = bump("h32", [128, 32 * 512], F32)
    hq = bump("hq", [128, 32 * 512], BF16)
    xqfm = bump("xqfm", [128, 8 * 2048], BF16)
    wfc_l = bump("wfc_l", [128, 2 * 8 * 128], BF16)
    wpq_l = bump("wpq_l", [128, 2 * 1024], BF16)
    hqi = bump("hqi", [128, 2 * 512], I32)
    out_sb = bump("out_sb", [128, 2 * D], F32)
    B_END = ptr[0]

    PREP_END = A_END

    BASE = 16576
    LIMIT = nc.SBUF_PARTITION_SIZE_BYTES
    assert max(A_END, P2_END, B_END, PREP_END) + BASE <= LIMIT - 32, (
        A_END, P2_END, B_END, PREP_END, LIMIT)

    psb = [nc.alloc_psum_tensor(f"psb{k}", [128, 512], F32).ap() for k in range(8)]

    p = Prog()
    SD, SG, SC2, SV, SS, ST, SOUT, SP = "sD", "sG", "sC", "sV", "sS", "sT", "sO", "sP"

    def vop(fn, wait=()):
        p.op("vector", fn, wait=wait, inc=(SV, 1))

    def sop(fn, wait=()):
        p.op("scalar", fn, wait=wait, inc=(SS, 1))

    def top(fn, wait=()):
        p.op("tensor", fn, wait=wait, inc=(ST, 1))

    pa_v = {0: 0, 1: 0}
    pa_d = {0: 0, 1: 0}
    for j in range(12):
        sl = j % 2
        p.dma("sync", SD, wraw[:, sl * 1024:(sl + 1) * 1024],
              wqkv_d[j * 128:(j + 1) * 128, :], wait=[(SV, pa_v[sl])])
        dld = p.v(SD)
        vop(lambda e, s=sl: e.tensor_copy(out=wqbf[:, s * 1024:(s + 1) * 1024],
                                          in_=wraw[:, s * 1024:(s + 1) * 1024]),
            wait=[(SD, dld), (SD, pa_d[sl])])
        pa_v[sl] = p.v(SV)
        p.dma("sync", SD,
              wqkvT.rearrange("q (d c) -> q d c", c=1536)[:, :, j * 128:(j + 1) * 128],
              wqbf[:, sl * 1024:(sl + 1) * 1024].rearrange("q (d c) -> q d c", c=128),
              wait=[(SV, pa_v[sl])], transpose=True)
        pa_d[sl] = p.v(SD)
    wq_tr = p.v(SD)
    for jj in range(4):
        sl = jj % 2
        p.dma("sync", SD, wraw[:, sl * 1024:(sl + 1) * 1024].rearrange("q (a c) -> q a c", c=512),
              wout_d.rearrange("(j q) c -> q j c", q=128)[:, 2 * jj:2 * jj + 2, :],
              wait=[(SV, pa_v[sl])])
        dld = p.v(SD)
        vop(lambda e, s=sl: e.tensor_copy(out=wqbf[:, s * 1024:(s + 1) * 1024],
                                          in_=wraw[:, s * 1024:(s + 1) * 1024]),
            wait=[(SD, dld), (SD, pa_d[sl])])
        pa_v[sl] = p.v(SV)
        for k2 in range(2):
            j = 2 * jj + k2
            p.dma("sync", SD,
                  woutT.rearrange("q (d c) -> q d c", c=1024)[:, :, j * 128:(j + 1) * 128],
                  wqbf[:, sl * 1024 + k2 * 512:sl * 1024 + (k2 + 1) * 512].rearrange("q (d c) -> q d c", c=128),
                  wait=[(SV, pa_v[sl])] if k2 == 0 else [], transpose=True)
        pa_d[sl] = p.v(SD)
    wo_tr = p.v(SD)
    p.dma("sync", SD, brow2[0:1, 0:D], bout_d)
    p.dma("sync", SD, brow2[0:1, D:2 * D], bpj_d)
    p.dma("sync", SD, m0_t, m0_d)
    p.dma("sync", SD, m1_t, m1_d)
    p.dma("sync", SD, onc, onc_d)
    p.dma("sync", SD, onr, onr_d)
    p.dma("sync", SD, o11, o11_d)
    p.dma("sync", SD, kon, kon_d)
    p.dma("sync", SD, bfc_cols, bfc_d.rearrange("o (q t) -> (o q) t", q=128))
    c_ld = p.v(SD)
    vop(lambda e: e.tensor_tensor(out=brow2[0:1, 0:D], in0=brow2[0:1, 0:D],
                                  in1=brow2[0:1, D:2 * D], op=ALU.add), wait=[(SD, c_ld)])
    bs_v = p.v(SV)
    for h in range(2):
        top(lambda e, hh=h: e.matmul(psb[6 + hh], onr, brow2[0:1, hh * 512:(hh + 1) * 512],
                                     start=True, stop=True), wait=[(SV, bs_v)])
    bs_t = p.v(ST)
    for h in range(2):
        vop(lambda e, hh=h: e.tensor_copy(out=bsum_bc[:, hh * 512:(hh + 1) * 512],
                                          in_=psb[6 + hh]), wait=[(ST, bs_t)] if h == 0 else [])
    PREPA = {"v": p.v(SV), "d": p.v(SD), "t": p.v(ST), "s": p.v(SS)}

    chain_g = [0]

    def sw():
        return [(SV, p.v(SV))]

    def scale_chain(nb):
        sop(lambda e, n=nb: e.activation(out=c_t[:, 0:n], in_=c_am[:, 0:n], func=AF.Ln,
                                         bias=K_TINY), wait=[(SV, p.v(SV))])
        sv = p.v(SS)
        vop(lambda e, n=nb: e.tensor_scalar(out=c_t[:, 0:n], in0=c_t[:, 0:n],
                                            scalar1=1.0 / LN2, scalar2=-3.0,
                                            op0=ALU.mult, op1=ALU.add), wait=[(SS, sv)])
        vop(lambda e, n=nb: e.tensor_copy(out=c_i[:, 0:n], in_=c_t[:, 0:n]), wait=sw())
        vop(lambda e, n=nb: e.tensor_copy(out=c_f[:, 0:n], in_=c_i[:, 0:n]), wait=sw())
        vop(lambda e, n=nb: e.tensor_tensor(out=c_am[:, 0:n], in0=c_f[:, 0:n],
                                            in1=c_t[:, 0:n], op=ALU.is_gt), wait=sw())
        vop(lambda e, n=nb: e.tensor_tensor(out=c_f[:, 0:n], in0=c_f[:, 0:n],
                                            in1=c_am[:, 0:n], op=ALU.subtract), wait=sw())
        vv = p.v(SV)
        sop(lambda e, n=nb: e.activation(out=c_s[:, 0:n], in_=c_f[:, 0:n], func=AF.Exp,
                                         scale=K_LN2), wait=[(SV, vv)])
        sop(lambda e, n=nb: e.activation(out=c_si[:, 0:n], in_=c_f[:, 0:n], func=AF.Exp,
                                         scale=K_NLN2))
        sv = p.v(SS)
        vop(lambda e, n=nb: e.tensor_copy(out=c_bf[:, 0:n], in_=c_s[:, 0:n]), wait=[(SS, sv)])
        vop(lambda e, n=nb: e.tensor_copy(out=c_s[:, 0:n], in_=c_bf[:, 0:n]), wait=sw())
        vop(lambda e, n=nb: e.tensor_copy(out=c_bf[:, 0:n], in_=c_si[:, 0:n]), wait=sw())
        vop(lambda e, n=nb: e.tensor_copy(out=c_si[:, 0:n], in_=c_bf[:, 0:n]), wait=sw())
        chain_g[0] = p.v(SS)

    def quant_piece(raw_sl, out_sl):
        scale_chain(32)
        for hf in range(2):
            vop(lambda e, HF=hf, s=raw_sl: e.tensor_tensor(
                out=c_i[:, 0:512].rearrange("q (b w) -> q b w", w=32),
                in0=wraw[:, s * 1024 + HF * 512:s * 1024 + (HF + 1) * 512].rearrange("q (b w) -> q b w", w=32),
                in1=c_si[:, HF * 16:(HF + 1) * 16].unsqueeze(2).broadcast_to([128, 16, 32]),
                op=ALU.mult), wait=sw())
            vop(lambda e, HF=hf, s=out_sl: e.tensor_tensor(
                out=wqbf[:, s * 1024 + HF * 512:s * 1024 + (HF + 1) * 512].rearrange("q (b w) -> q b w", w=32),
                in0=c_i[:, 0:512].rearrange("q (b w) -> q b w", w=32),
                in1=c_s[:, HF * 16:(HF + 1) * 16].unsqueeze(2).broadcast_to([128, 16, 32]),
                op=ALU.mult), wait=sw())

    def prep_b_gen():
        g_raw = {0: 0, 1: 0}
        g_bf_sg = {0: 0, 1: 0}
        g_bf_sd = {0: 0, 1: 0}
        first = True
        for j in range(32):
            sl = j % 2
            w = [(SV, g_raw[sl])]
            if first:
                w += [(SV, PREPA["v"]), (SD, PREPA["d"])]
            p.dma("gpsimd", SG, wraw[:, sl * 1024:(sl + 1) * 1024],
                  wfc_d[j * 128:(j + 1) * 128, :], wait=w)
            first = False
            dld = ("G", p.v(SG))
            vop(lambda e, s=sl: e.tensor_reduce(
                out=c_am[:, 0:32],
                in_=wraw[:, s * 1024:(s + 1) * 1024].rearrange("q (b w) -> q b w", w=32),
                axis=AX.X, op=ALU.max, apply_absolute_value=True),
                wait=[(SG, dld[1]), (SS, chain_g[0]), (SG, g_bf_sg[sl]),
                      (SD, g_bf_sd[sl])])
            quant_piece(sl, sl)
            g_raw[sl] = p.v(SV)
            p.dma("gpsimd", SG, wfcq_s[j * 128:(j + 1) * 128, :],
                  wqbf[:, sl * 1024:(sl + 1) * 1024], wait=[(SV, p.v(SV))])
            g_bf_sg[sl] = p.v(SG)
            yield
        g_tr = 0
        for j in range(8):
            for q4 in range(4):
                sl = (j * 4 + q4) % 2
                p.dma("sync", SD, wraw[:, sl * 1024:(sl + 1) * 1024],
                      wpj_d[j * 128:(j + 1) * 128, q4 * 1024:(q4 + 1) * 1024],
                      wait=[(SV, g_raw[sl])])
                dld = p.v(SD)
                vop(lambda e, s=sl: e.tensor_reduce(
                    out=c_am[:, 0:32],
                    in_=wraw[:, s * 1024:(s + 1) * 1024].rearrange("q (b w) -> q b w", w=32),
                    axis=AX.X, op=ALU.max, apply_absolute_value=True),
                    wait=[(SD, dld), (SS, chain_g[0]), (SG, g_bf_sg[sl]),
                          (SD, g_bf_sd[sl])])
                quant_piece(sl, sl)
                g_raw[sl] = p.v(SV)
                p.dma("sync", SD, wtst.rearrange("q (b c) -> q b c", c=128),
                      wqbf[:, sl * 1024:(sl + 1) * 1024].rearrange("q (b c) -> q b c", c=128),
                      wait=[(SV, p.v(SV)), (SG, g_tr)], transpose=True)
                tr_d = p.v(SD)
                g_bf_sd[sl] = tr_d
                p.dma("gpsimd", SG,
                      wpqT_s.rearrange("(b q) c -> q b c", q=128)[:, 8 * q4:8 * q4 + 8, j * 128:(j + 1) * 128],
                      wtst.rearrange("q (b c) -> q b c", c=128), wait=[(SD, tr_d)])
                g_tr = p.v(SG)
                yield

    PB = prep_b_gen()

    def pb(n=1):
        for _ in range(n):
            next(PB, None)

    xsA = {}
    for t in range(NT):
        sl = t % 2
        w = [(SV, xsA.get(("v", sl), 0)), (SS, xsA.get(("s", sl), 0))]
        p.dma("sync", SD, xstream[:, sl * D:(sl + 1) * D],
              xf_d[t * 128:(t + 1) * 128, :], wait=w)
        xld = p.v(SD)
        vop(lambda e, s=sl, tt=t: e.reduce_sum(
            out=ssum[:, tt:tt + 1], in_=xstream[:, s * D:(s + 1) * D], axis=AX.X),
            wait=[(SD, xld)])
        xsA[("v", sl)] = p.v(SV)
        sop(lambda e, s=sl, tt=t: e.activation(
            out=sqdump, in_=xstream[:, s * D:(s + 1) * D], func=AF.Square,
            accum_out=ssq[:, tt:tt + 1]), wait=[(SD, xld)])
        sop(lambda e: e.activation(out=sdum, in_=kon[:, 7:8], func=AF.Square))
        xsA[("s", sl)] = p.v(SS)
        pb(1)
    sv = p.v(SS)
    vop(lambda e: e.tensor_scalar(out=st1, in0=ssq, scalar1=1.0 / D,
                                  scalar2=None, op0=ALU.mult), wait=[(SS, sv)] + sw())
    vop(lambda e: e.tensor_scalar(out=negm, in0=ssum, scalar1=-1.0 / D,
                                  scalar2=None, op0=ALU.mult), wait=sw())
    vop(lambda e: e.tensor_tensor(out=st2, in0=negm, in1=negm, op=ALU.mult), wait=sw())
    vop(lambda e: e.tensor_tensor(out=st1, in0=st1, in1=st2, op=ALU.subtract), wait=sw())
    vv = p.v(SV)
    sop(lambda e: e.activation(out=st2, in_=st1, func=AF.Ln, bias=K_EPS), wait=[(SV, vv)])
    sop(lambda e: e.activation(out=rstd, in_=st2, func=AF.Exp, scale=K_NHALF))
    STATS_S = p.v(SS)

    xsB = {}
    xnbf_g = {0: 0, 1: 0}
    xnfm_g = {0: 0, 1: 0, 2: 0, 3: 0}
    bank_g = {}
    vtr_g = [0]
    p0_tr = {}

    def emit_pB_chunk(c):
        for k in range(2):
            t = 2 * c + k
            sl = t % 2
            w = [(SV, xsB.get(("v", sl), xsA.get(("v", sl), 0)))]
            if c == 0:
                w += [(SS, STATS_S)]
            p.dma("sync", SD, xstream[:, sl * D:(sl + 1) * D],
                  xf_d[t * 128:(t + 1) * 128, :], wait=w)
            xld = p.v(SD)
            wslot = t % 4
            vop(lambda e, s=sl, tt=t: e.tensor_scalar(
                out=xnbf[:, s * D:(s + 1) * D], in0=xstream[:, s * D:(s + 1) * D],
                scalar1=negm[:, tt:tt + 1], scalar2=rstd[:, tt:tt + 1],
                op0=ALU.add, op1=ALU.mult),
                wait=[(SD, xld), (SS, STATS_S), (SD, xnbf_g[sl])])
            xn_v = p.v(SV)
            wtr = [(SV, xn_v), (ST, xnfm_g[wslot])]
            if c == 0 and k == 0:
                wtr.append((ST, PREPA["t"]))
            p.dma("scalar", SC2,
                  xnfm.rearrange("q (d c) -> q d c", c=512)[:, :, wslot * 128:(wslot + 1) * 128],
                  xnbf[:, sl * D:(sl + 1) * D].rearrange("q (d c) -> q d c", c=128),
                  wait=wtr, transpose=True)
            xnbf_g[sl] = p.v(SC2)
            xsB[("v", sl)] = p.v(SV)
        p0_tr[c] = p.v(SC2)

    def emit_p1_chunk(c):
        s0 = (2 * c) % 4
        tr_done = p0_tr[c]
        for ec in range(4):
            bk = ec % 2
            for dmc in range(8):
                rr = xnfm[:, dmc * 512 + s0 * 128:dmc * 512 + s0 * 128 + 256]
                ll = wqkvT[:, dmc * 1536 + 512 + ec * 128:dmc * 1536 + 640 + ec * 128]
                w = [(SC2, tr_done), (SV, bank_g.get(bk, 0))] if dmc == 0 else []
                if dmc == 0 and ec == 0 and c == 0:
                    w.append((SD, wq_tr))
                top(lambda e, L=ll, R=rr, B=bk, M=dmc: e.matmul(
                    psb[B][:, 0:256], L, R, start=(M == 0), stop=(M == 7)), wait=w)
            mm = p.v(ST)
            vop(lambda e, E=ec, B=bk, C=c: e.tensor_copy(
                out=kfm[:, E * S + C * 256:E * S + (C + 1) * 256], in_=psb[B][:, 0:256]),
                wait=[(ST, mm)])
            bank_g[bk] = p.v(SV)
        for ec in range(4):
            bk = ec % 2
            for dmc in range(8):
                rr = xnfm[:, dmc * 512 + s0 * 128:dmc * 512 + s0 * 128 + 256]
                ll = wqkvT[:, dmc * 1536 + 1024 + ec * 128:dmc * 1536 + 1152 + ec * 128]
                w = [(SV, bank_g.get(bk, 0))] if dmc == 0 else []
                if dmc == 0 and ec == 0:
                    w.append((SC2, vtr_g[0]))
                top(lambda e, L=ll, R=rr, B=bk, M=dmc: e.matmul(
                    psb[B][:, 0:256], L, R, start=(M == 0), stop=(M == 7)), wait=w)
            mm = p.v(ST)
            vop(lambda e, E=ec, B=bk: e.tensor_copy(
                out=vfmb[:, E * 256:(E + 1) * 256], in_=psb[B][:, 0:256]), wait=[(ST, mm)])
            bank_g[bk] = p.v(SV)
        vv2 = p.v(SV)
        for ec in range(4):
            p.dma("scalar", SC2,
                  vtm.rearrange("q (a c) -> q a c", c=512)[:, 2 * c:2 * c + 2, ec * 128:(ec + 1) * 128],
                  vfmb[:, ec * 256:(ec + 1) * 256].rearrange("q (a c) -> q a c", c=128),
                  wait=[(SV, vv2)] if ec == 0 else [], transpose=True)
        vtr_g[0] = p.v(SC2)
        for ec in range(4):
            bk = 2 + ec % 2
            for dmc in range(8):
                rr = xnfm[:, dmc * 512 + s0 * 128:dmc * 512 + s0 * 128 + 128]
                ll = wqkvT[:, dmc * 1536 + ec * 128:dmc * 1536 + (ec + 1) * 128]
                w = [(SV, bank_g.get(bk, 0))] if dmc == 0 else []
                top(lambda e, L=ll, R=rr, B=bk, M=dmc: e.matmul(
                    psb[B][:, 0:128], L, R, start=(M == 0), stop=(M == 7)), wait=w)
            mm = p.v(ST)
            vop(lambda e, E=ec, B=bk, C=c: e.tensor_copy(
                out=qfm[:, E * 2048 + C * 128:E * 2048 + (C + 1) * 128], in_=psb[B][:, 0:128]),
                wait=[(ST, mm)])
            bank_g[bk] = p.v(SV)
        xnfm_g[s0] = p.v(ST)
        xnfm_g[(s0 + 1) % 4] = p.v(ST)

    for c in range(NCH):
        emit_pB_chunk(c)
        if c >= 1:
            emit_p1_chunk(c - 1)
    emit_p1_chunk(NCH - 1)
    P1 = {"v": p.v(SV), "d": p.v(SD), "t": p.v(ST), "g": p.v(SG), "c": p.v(SC2)}

    pc_state = {}

    def pass_c_gen():
        xc_g = 0
        xq_g = {}
        for i in range(NOWN):
            t = 2 * i
            w = [(SV, xc_g)]
            if i == 0:
                w += [(SV, P1["v"]), (ST, P1["t"]), (SD, P1["d"])]
            p.dma("sync", SD, xcload, xo_d[i * 128:(i + 1) * 128, :], wait=w)
            xld = p.v(SD)
            vop(lambda e, tt=t: e.tensor_scalar(
                out=xn32, in0=xcload,
                scalar1=negm[:, tt:tt + 1], scalar2=rstd[:, tt:tt + 1],
                op0=ALU.add, op1=ALU.mult), wait=[(SD, xld)] + sw())
            xc_g = p.v(SV)
            vop(lambda e, I=i: e.tensor_reduce(
                out=c_am[:, 32 * I:32 * I + 32],
                in_=xn32.rearrange("q (b w) -> q b w", w=32),
                axis=AX.X, op=ALU.max, apply_absolute_value=True),
                wait=([(SS, chain_g[0])] if i == 0 else []))
            yield
        scale_chain(512)
        yield
        for i in range(NOWN):
            t = 2 * i
            ssl = i % 2
            p.dma("sync", SD, xcload, xo_d[i * 128:(i + 1) * 128, :], wait=[(SV, xc_g)])
            xld = p.v(SD)
            vop(lambda e, tt=t: e.tensor_scalar(
                out=xn32, in0=xcload,
                scalar1=negm[:, tt:tt + 1], scalar2=rstd[:, tt:tt + 1],
                op0=ALU.add, op1=ALU.mult), wait=[(SD, xld)] + sw())
            xc_g = p.v(SV)
            for hf in range(2):
                vop(lambda e, HF=hf, I=i: e.tensor_tensor(
                    out=c_i[:, 0:512].rearrange("q (b w) -> q b w", w=32),
                    in0=xn32[:, HF * 512:(HF + 1) * 512].rearrange("q (b w) -> q b w", w=32),
                    in1=c_si[:, 32 * I + HF * 16:32 * I + (HF + 1) * 16].unsqueeze(2).broadcast_to([128, 16, 32]),
                    op=ALU.mult), wait=sw())
                vop(lambda e, HF=hf, I=i: e.tensor_tensor(
                    out=xqbf[:, HF * 512:(HF + 1) * 512].rearrange("q (b w) -> q b w", w=32),
                    in0=c_i[:, 0:512].rearrange("q (b w) -> q b w", w=32),
                    in1=c_s[:, 32 * I + HF * 16:32 * I + (HF + 1) * 16].unsqueeze(2).broadcast_to([128, 16, 32]),
                    op=ALU.mult),
                    wait=[(SD, xq_g.get(("tr", 1 - ssl), 0))] if hf == 0 else [])
            xq_v = p.v(SV)
            p.dma("sync", SD,
                  xqst[:, ssl * 1024:(ssl + 1) * 1024].rearrange("q (d c) -> q d c", c=128),
                  xqbf.rearrange("q (d c) -> q d c", c=128),
                  wait=[(SV, xq_v), (SG, xq_g.get(("sp", ssl), 0))], transpose=True)
            xq_g[("tr", ssl)] = p.v(SD)
            p.dma("gpsimd", SG,
                  xq_s.rearrange("q (d c) -> q d c", c=2048)[:, :, i * 128:(i + 1) * 128],
                  xqst[:, ssl * 1024:(ssl + 1) * 1024].rearrange("q (d c) -> q d c", c=128),
                  wait=[(SD, xq_g[("tr", ssl)])])
            xq_g[("sp", ssl)] = p.v(SG)
            yield
        pc_state["done"] = True

    PC = pass_c_gen()
    pc_state["done"] = False

    sc_rd = {4: 0, 5: 0}
    expP_rd_v = {0: 0, 1: 0}
    expP_rd_t = {0: 0, 1: 0}
    av_prev_ev = 0
    prev_acc_v = 0
    denom_rd_t = 0
    rc_v = 0
    xpb_v_free = 0

    for g in range(4):
        ktmax = 8 * g + 8
        gsl = g % 2

        def emit_scores(kt, g=g, gsl=gsl):
            cstart = max(0, kt // 2 - 4 * g) * 128
            cols = 512 - cstart
            qcol0 = g * 512 + cstart
            sb = 4 + (kt % 2)
            for dc in range(4):
                w = []
                if dc == 0:
                    w = [(SS, sc_rd[sb])]
                    if kt == 0 and g == 0:
                        w += [(SV, P1["v"]), (SD, P1["d"]), (SC2, P1["c"])]
                ll = kfm[:, dc * S + kt * 128:dc * S + (kt + 1) * 128]
                rr = qfm[:, dc * 2048 + qcol0:dc * 2048 + qcol0 + cols]
                top(lambda e, L=ll, R=rr, B=sb, DC=dc, CO=cols: e.matmul(
                    psb[B][:, 0:CO], L, R, start=(DC == 0), stop=(DC == 3)), wait=w)
            return p.v(ST), cols, cstart

        sc_done, cols, cstart = emit_scores(0)
        for kt in range(ktmax):
            esl = kt % 2
            sb = 4 + esl
            sop(lambda e, B=sb, CO=cols, E=esl: e.activation(
                out=expP[:, E * 512:E * 512 + CO], in_=psb[B][:, 0:CO],
                func=AF.Exp, bias=K_NCSH, scale=K_ASC),
                wait=[(ST, sc_done), (SP, expP_rd_v[esl]), (ST, expP_rd_t[esl])])
            sc_rd[sb] = p.v(SS)
            exp_done = p.v(SS)
            wv = [(SS, exp_done)]
            if kt >= 8 * g:
                mt = m0_t if kt % 2 == 0 else m1_t
                p.op("gpsimd", lambda e, E=esl, M=mt: e.tensor_tensor(
                    out=expP[:, E * 512:E * 512 + 128], in0=expP[:, E * 512:E * 512 + 128],
                    in1=M, op=ALU.mult), wait=[(SS, exp_done)], inc=(SP, 1))
                wv.append((SP, p.v(SP)))
            expP_rd_v[esl] = p.v(SP)
            cs_cur, co_cur = cstart, cols
            if kt + 1 < ktmax:
                sc_done, cols, cstart = emit_scores(kt + 1)
            for dc in range(4):
                ll = vtm[:, kt * 512 + dc * 128:kt * 512 + (dc + 1) * 128]
                w = list(wv) if dc == 0 else []
                if dc == 0 and kt == 0:
                    w += [(SV, prev_acc_v), (SV, av_prev_ev)]
                top(lambda e, L=ll, DC=dc, CS=cs_cur, CO=co_cur, E=esl, F=(kt == 0),
                    LAST=(kt == ktmax - 1): e.matmul(
                    psb[DC][:, CS:CS + CO], L, expP[:, E * 512:E * 512 + CO],
                    start=F, stop=LAST, skip_group_check=True), wait=w)
            top(lambda e, CS=cs_cur, CO=co_cur, E=esl, F=(kt == 0),
                LAST=(kt == ktmax - 1): e.matmul(
                psb[6][0:1, CS:CS + CO], onc, expP[:, E * 512:E * 512 + CO],
                start=F, stop=LAST, skip_group_check=True))
            expP_rd_t[esl] = p.v(ST)
            if not pc_state["done"]:
                next(PC, None)
            else:
                pb(2)
        av_mm = p.v(ST)
        vop(lambda e: e.tensor_copy(out=denom_r, in_=psb[6][0:1, 0:512]),
            wait=[(ST, av_mm), (ST, denom_rd_t)])
        for dc in range(4):
            w = [(ST, av_mm)] if dc == 0 else []
            vop(lambda e, DC=dc: e.tensor_copy(
                out=avun[:, DC * 512:(DC + 1) * 512], in_=psb[DC]), wait=w)
        av_prev_ev = p.v(SV)
        dn_v = p.v(SV)
        for ti in range(4):
            i = 4 * g + ti
            top(lambda e, TI=ti: e.matmul(psb[7][:, 0:1],
                                          denom_r[0:1, TI * 128:(TI + 1) * 128], o11,
                                          start=True, stop=True),
                wait=[(SV, dn_v), (SV, rc_v)])
            vop(lambda e, I=i: e.reciprocal(out=recips[:, I:I + 1], in_=psb[7][:, 0:1]),
                wait=[(ST, p.v(ST))])
            rc_v = p.v(SV)
            p.dma("sync", SD, xpb, xo_d[i * 128:(i + 1) * 128, :], wait=[(SV, xpb_v_free)])
            vop(lambda e: e.tensor_tensor(out=xpb, in0=xpb, in1=bsum_bc, op=ALU.add),
                wait=[(SD, p.v(SD))])
            xpb_v = p.v(SV)
            for dc in range(4):
                for oh in range(2):
                    w = []
                    if dc == 0 and oh == 0:
                        w = [(SV, av_prev_ev), (SV, prev_acc_v)]
                        if g == 0 and ti == 0:
                            w.append((SD, wo_tr))
                    ll = avun[:, dc * 512 + ti * 128:dc * 512 + (ti + 1) * 128]
                    rr2 = woutT[:, dc * D + oh * 512:dc * D + (oh + 1) * 512]
                    top(lambda e, L=ll, R=rr2, OH=oh, DC=dc: e.matmul(
                        psb[OH], L, R, start=(DC == 0), stop=(DC == 3)), wait=w)
            opmm = p.v(ST)
            for oh in range(2):
                vop(lambda e, OH=oh, I=i: e.scalar_tensor_tensor(
                    out=acc[:, I * D + OH * 512:I * D + (OH + 1) * 512],
                    in0=psb[OH], scalar=recips[:, I:I + 1],
                    in1=xpb[:, OH * 512:(OH + 1) * 512],
                    op0=ALU.mult, op1=ALU.add),
                    wait=([(ST, opmm), (SV, xpb_v)] + sw()) if oh == 0 else [])
            prev_acc_v = p.v(SV)
            xpb_v_free = p.v(SV)
        denom_rd_t = p.v(ST)
    for _ in PC:
        pass
    for _ in PB:
        pass
    P2 = {"v": p.v(SV), "s": p.v(SS), "t": p.v(ST), "d": p.v(SD), "g": p.v(SG)}
    if dbg:
        p.dma("gpsimd", SG, dbg_acc, acc, wait=[(SV, P2["v"])])
        p.dma("gpsimd", SG, dbg_rec, recips)

    p.dma("sync", SD, xqfm, xq_s, wait=[(SV, P2["v"]), (ST, P2["t"]), (SS, P2["s"]),
                                        (SG, P2["g"])])
    xq_ld = p.v(SD)
    wfc_rows = wfcq_s.rearrange("(q s) c -> s q c", s=32)
    wpq_rows = wpqT_s.rearrange("(q s) c -> s q c", s=32)
    wfc_g = {0: 0, 1: 0}
    wpq_g = {0: 0, 1: 0}
    fc_rd = {0: 0, 1: 0}
    pj_guard = 0
    h32_rd_prev = 0
    hq_rd_prev_t = 0
    out_sb_g = {0: 0, 1: 0}
    for sp in range(4):
        for hc in range(32):
            sl = hc % 2
            w = [(ST, wfc_g[sl])]
            if sp == 0 and hc == 0:
                w += [(SD, xq_ld), (ST, P2["t"]), (SG, P2["g"])]
            p.dma("sync", SD,
                  wfc_l[:, sl * 1024:(sl + 1) * 1024].rearrange("q (b c) -> q b c", c=128),
                  wfc_rows[hc].rearrange("q (b c) -> q b c", c=128),
                  wait=w, transpose=True)
            trd = p.v(SD)
            bk = hc % 2
            for dmc in range(8):
                w = []
                if dmc == 0:
                    w = [(SD, trd), (SS, fc_rd[bk])]
                    if sp == 0 and hc < 2:
                        w += [(SV, P2["v"])]
                ll = wfc_l[:, sl * 1024 + dmc * 128:sl * 1024 + (dmc + 1) * 128]
                rr = xqfm[:, dmc * 2048 + sp * 512:dmc * 2048 + (sp + 1) * 512]
                top(lambda e, L=ll, R=rr, B=bk, M=dmc: e.matmul(
                    psb[B], L, R, start=(M == 0), stop=(M == 7)), wait=w)
            wfc_g[sl] = p.v(ST)
            mmv = p.v(ST)
            w = [(ST, mmv)]
            if hc == 0:
                w.append((SV, h32_rd_prev))
            sop(lambda e, HC=hc, B=bk: e.activation(
                out=h32[:, HC * 512:(HC + 1) * 512], in_=psb[B],
                func=AF.Gelu, bias=bfc_cols[:, HC:HC + 1]), wait=w)
            fc_rd[bk] = p.v(SS)
            w = [(SS, fc_rd[bk])]
            if hc == 0:
                w.append((SS, chain_g[0]))
            vop(lambda e, HC=hc, FIRST=(hc == 0): e.tensor_tensor(
                out=c_am[:, 0:512],
                in0=(h32[:, HC * 512:(HC + 1) * 512] if FIRST else c_am[:, 0:512]),
                in1=h32[:, HC * 512:(HC + 1) * 512], op=ALU.max), wait=w)
            vop(lambda e, HC=hc, FIRST=(hc == 0): e.tensor_tensor(
                out=c_t[:, 0:512],
                in0=(h32[:, HC * 512:(HC + 1) * 512] if FIRST else c_t[:, 0:512]),
                in1=h32[:, HC * 512:(HC + 1) * 512], op=ALU.min))
        vop(lambda e: e.tensor_scalar(out=c_t[:, 0:512], in0=c_t[:, 0:512],
                                      scalar1=-1.0, scalar2=None, op0=ALU.mult),
            wait=sw())
        vop(lambda e: e.tensor_tensor(out=c_am[:, 0:512], in0=c_am[:, 0:512],
                                      in1=c_t[:, 0:512], op=ALU.max), wait=sw())
        scale_chain(512)
        hq_done = {}
        for hc in range(32):
            qsl = hc % 2
            w = sw()
            if hc == 0:
                w += [(ST, hq_rd_prev_t)]
            vop(lambda e, HC=hc, QS=qsl: e.tensor_tensor(
                out=hqi[:, QS * 512:(QS + 1) * 512], in0=h32[:, HC * 512:(HC + 1) * 512],
                in1=c_si[:, 0:512], op=ALU.mult), wait=w)
            vop(lambda e, HC=hc, QS=qsl: e.tensor_tensor(
                out=hq[:, HC * 512:(HC + 1) * 512], in0=hqi[:, QS * 512:(QS + 1) * 512],
                in1=c_s[:, 0:512], op=ALU.mult))
            hq_done[hc] = p.v(SV)
        h32_rd_prev = p.v(SV)
        for pair in range(2):
            for hc in range(32):
                sl = hc % 2
                w = [(ST, wpq_g[sl])]
                if sp == 0 and pair == 0 and hc < 2:
                    w += [(ST, P2["t"]), (SG, P2["g"])]
                p.dma("scalar", SC2, wpq_l[:, sl * 1024:(sl + 1) * 1024], wpq_rows[hc],
                      wait=w)
                wld = p.v(SC2)
                for ti in range(2):
                    ilocal = pair * 2 + ti
                    for oh in range(2):
                        w = []
                        if ti == 0 and oh == 0:
                            w = [(SC2, wld), (SV, hq_done[hc if pair == 0 else 31])]
                            if hc == 0:
                                w += [(SV, pj_guard)]
                        ll = hq[:, hc * 512 + ilocal * 128:hc * 512 + (ilocal + 1) * 128]
                        rr2 = wpq_l[:, sl * 1024 + oh * 512:sl * 1024 + (oh + 1) * 512]
                        top(lambda e, L=ll, R=rr2, TI=ti, OH=oh, HC=hc: e.matmul(
                            psb[2 + TI * 2 + OH], L, R, start=(HC == 0), stop=(HC == 31)),
                            wait=w)
                wpq_g[sl] = p.v(ST)
            pjmm = p.v(ST)
            for ti in range(2):
                i = 4 * sp + pair * 2 + ti
                osl = ti
                for oh in range(2):
                    w = []
                    if oh == 0:
                        w = [(ST, pjmm), (SOUT, out_sb_g[osl])]
                    vop(lambda e, TI=ti, OH=oh, I=i, OS=osl: e.tensor_tensor(
                        out=out_sb[:, OS * D + OH * 512:OS * D + (OH + 1) * 512],
                        in0=psb[2 + TI * 2 + OH],
                        in1=acc[:, I * D + OH * 512:I * D + (OH + 1) * 512],
                        op=ALU.add), wait=w)
                ev = p.v(SV)
                p.dma("gpsimd", SOUT, y_d[i * 128:(i + 1) * 128, :],
                      out_sb[:, osl * D:(osl + 1) * D], wait=[(SV, ev)])
                out_sb_g[osl] = p.v(SOUT)
            pj_guard = p.v(SV)
        hq_rd_prev_t = p.v(ST)

    if dbg:
        p.dma("gpsimd", SG, dbg_h32, h32, wait=[(SV, p.v(SV)), (SS, p.v(SS))])
        p.dma("gpsimd", SG, dbg_hq, hq)
        p.op("gpsimd", lambda e: e.engine_nop(), wait=[(SG, p.v(SG))])
    p.op("gpsimd", lambda e: e.engine_nop(), wait=[(SOUT, p.v(SOUT))])

    from contextlib import ExitStack
    with ExitStack() as ctx:
        sems = {}
        for s in [SD, SG, SC2, SV, SS, ST, SOUT, SP]:
            sems[s] = ctx.enter_context(nc.semaphore(s))
        block = ctx.enter_context(nc.Block())

        @block.sync
        def _(eng):
            run_stream(p, "sync", eng, sems)

        @block.vector
        def _(eng):
            run_stream(p, "vector", eng, sems)

        @block.scalar
        def _(eng):
            run_stream(p, "scalar", eng, sems)

        @block.tensor
        def _(eng):
            run_stream(p, "tensor", eng, sems)

        @block.gpsimd
        def _(eng):
            run_stream(p, "gpsimd", eng, sems)
    return nc


_NC_CACHE = [None]


def kernel(L0_attn_x, L0_attn_w_qkv, L0_attn_w_out, L0_attn_b_out,
           L0_ffn_w_fc, L0_ffn_b_fc, L0_ffn_w_proj, L0_ffn_b_proj):
    x = np.ascontiguousarray(np.asarray(L0_attn_x, dtype=np.float32))
    wqkv = np.ascontiguousarray(np.asarray(L0_attn_w_qkv, dtype=np.float32))
    wout = np.ascontiguousarray(np.asarray(L0_attn_w_out, dtype=np.float32))
    bout = np.ascontiguousarray(np.asarray(L0_attn_b_out, dtype=np.float32)).reshape(1, D)
    wfc = np.ascontiguousarray(np.asarray(L0_ffn_w_fc, dtype=np.float32))
    bfc = np.ascontiguousarray(np.asarray(L0_ffn_b_fc, dtype=np.float32)).reshape(1, HID)
    wpj = np.ascontiguousarray(np.asarray(L0_ffn_w_proj, dtype=np.float32))
    bpj = np.ascontiguousarray(np.asarray(L0_ffn_b_proj, dtype=np.float32)).reshape(1, D)

    if _NC_CACHE[0] is None:
        _NC_CACHE[0] = build()
    nc = _NC_CACHE[0]

    ii, jj = np.meshgrid(np.arange(128), np.arange(128), indexing="ij")
    tril_keep = (ii <= jj).astype(ml_dtypes.bfloat16)
    zeros = np.zeros((128, 128), ml_dtypes.bfloat16)
    ones = np.ones((128, 128), ml_dtypes.bfloat16)
    ones_col = np.ones((128, 1), ml_dtypes.bfloat16)
    ones_row = np.ones((1, 128), np.float32)
    one11 = np.ones((1, 1), np.float32)
    konst = np.tile(np.array([[TINY, EPS, -CSH, LN2, -LN2, -0.5, ATT_SC, 0.0]],
                             np.float32), (128, 1))

    swap = np.arange(NT).reshape(-1, 2)[:, ::-1].reshape(-1)
    in_maps = []
    for c in range(8):
        b, r = c // 2, c % 2
        xl = x[b].reshape(NT, 128, D)
        if r:
            xl = xl[swap]
        in_maps.append({
            "x_full": np.ascontiguousarray(xl.reshape(S, D)),
            "x_own": np.ascontiguousarray(xl[0::2].reshape(NOWN * 128, D)),
            "wqkv": wqkv, "wout": wout, "bout": bout,
            "wfc": wfc, "bfc": bfc, "wproj": wpj, "bproj": bpj,
            "m0": np.ascontiguousarray(tril_keep),
            "m1": np.ascontiguousarray(ones if r else zeros),
            "ones_col": ones_col, "ones_row": ones_row, "one11": one11,
            "konst": konst,
        })
    res = run_bass_kernel_spmd(nc, in_maps, core_ids=list(range(8)))
    out = np.empty((4, S, D), np.float32)
    for c, rr in enumerate(res.results):
        b, r = c // 2, c % 2
        y = rr["y"].reshape(NOWN, 128, D)
        ob = out[b].reshape(NT, 128, D)
        for i in range(NOWN):
            ob[2 * i + r] = y[i]
    return out


# version 09
# speedup vs baseline: 1.0104x; 1.0104x over previous
import numpy as np
import ml_dtypes
import concourse.bass as bass
import concourse.mybir as mybir
from concourse.bass_utils import run_bass_kernel_spmd

F32 = mybir.dt.float32
I32 = mybir.dt.int32
BF16 = mybir.dt.bfloat16
AF = mybir.ActivationFunctionType
ALU = mybir.AluOpType
AX = mybir.AxisListType

S = 4096
D = 1024
DH = 512
HID = 4096
NT = 32
NOWN = 16
NCH = 16
ATT_SC = float(DH) ** -0.5
CSH = 15.0
LN2 = float(np.log(2.0))
TINY = 1e-37
EPS = 1e-5


class Prog:
    def __init__(self):
        self.ops = []
        self.cnt = {}

    def v(self, s):
        return self.cnt.get(s, 0)

    def op(self, eng, fn, wait=(), inc=None):
        if inc is not None:
            s, n = inc
            self.cnt[s] = self.cnt.get(s, 0) + n
        self.ops.append((eng, fn, tuple(wait), inc))

    def dma(self, eng, sem, out, in_, wait=(), **kw):
        self.op(eng, lambda e, o=out, i=in_, k=kw: e.dma_start(out=o, in_=i, **k),
                wait=wait, inc=(sem, 16))


def run_stream(prog, name, eng, sems):
    for e, fn, waits, inc in prog.ops:
        if e != name:
            continue
        for s, v in waits:
            if v > 0:
                eng.wait_ge(sems[s], v)
        ins = fn(eng)
        if inc is not None:
            ins.then_inc(sems[inc[0]], inc[1])


def build(dbg=False):
    nc = bass.Bass("TRN2", target_bir_lowering=False)

    xf_d = nc.dram_tensor("x_full", [S, D], F32, kind="ExternalInput").ap()
    xo_d = nc.dram_tensor("x_own", [NOWN * 128, D], F32, kind="ExternalInput").ap()
    wqkv_d = nc.dram_tensor("wqkv", [1536, D], F32, kind="ExternalInput").ap()
    wout_d = nc.dram_tensor("wout", [D, DH], F32, kind="ExternalInput").ap()
    bout_d = nc.dram_tensor("bout", [1, D], F32, kind="ExternalInput").ap()
    wfc_d = nc.dram_tensor("wfc", [HID, D], F32, kind="ExternalInput").ap()
    bfc_d = nc.dram_tensor("bfc", [1, HID], F32, kind="ExternalInput").ap()
    wpj_d = nc.dram_tensor("wproj", [D, HID], F32, kind="ExternalInput").ap()
    bpj_d = nc.dram_tensor("bproj", [1, D], F32, kind="ExternalInput").ap()
    m0_d = nc.dram_tensor("m0", [128, 128], BF16, kind="ExternalInput").ap()
    m1_d = nc.dram_tensor("m1", [128, 128], BF16, kind="ExternalInput").ap()
    onc_d = nc.dram_tensor("ones_col", [128, 1], BF16, kind="ExternalInput").ap()
    onr_d = nc.dram_tensor("ones_row", [1, 128], F32, kind="ExternalInput").ap()
    o11_d = nc.dram_tensor("one11", [1, 1], F32, kind="ExternalInput").ap()
    kon_d = nc.dram_tensor("konst", [128, 8], F32, kind="ExternalInput").ap()
    y_d = nc.dram_tensor("y", [NOWN * 128, D], F32, kind="ExternalOutput").ap()
    SCR = "ExternalOutput" if dbg else "Internal"
    wfcq_s = nc.dram_tensor("wfcq_s", [HID, D], BF16, kind=SCR).ap()
    wpqT_s = nc.dram_tensor("wpqT_s", [HID, D], BF16, kind=SCR).ap()
    xq_s = nc.dram_tensor("xq_s", [128, 8 * 2048], BF16, kind=SCR).ap()
    if dbg:
        dbg_kfm = nc.dram_tensor("dbg_kfm", [128, 4 * S], BF16, kind="ExternalOutput").ap()
        dbg_qfm = nc.dram_tensor("dbg_qfm", [128, 4 * 2048], BF16, kind="ExternalOutput").ap()
        dbg_vtm = nc.dram_tensor("dbg_vtm", [128, NT * DH], BF16, kind="ExternalOutput").ap()
        dbg_acc = nc.dram_tensor("dbg_acc", [128, NOWN * D], BF16, kind="ExternalOutput").ap()
        dbg_rec = nc.dram_tensor("dbg_rec", [128, NOWN], F32, kind="ExternalOutput").ap()
        dbg_stats = nc.dram_tensor("dbg_stats", [128, 4 * NT], F32, kind="ExternalOutput").ap()
        dbg_h32 = nc.dram_tensor("dbg_h32", [128, 32 * 512], F32, kind="ExternalOutput").ap()
        dbg_hq = nc.dram_tensor("dbg_hq", [128, 32 * 512], BF16, kind="ExternalOutput").ap()

    SZ = {F32: 4, I32: 4, BF16: 2}
    ptr = [0]

    BASE0 = 16576

    def bump(name, shape, dtype, align=64):
        pp = (ptr[0] + align - 1) // align * align
        nb = int(np.prod(shape[1:])) * SZ[dtype] if len(shape) > 1 else SZ[dtype]
        ptr[0] = pp + nb
        return nc.alloc_sbuf_tensor_at(name, list(shape), dtype, offset=BASE0 + pp).ap()

    ssum = bump("ssum", [128, NT], F32)
    ssq = bump("ssq", [128, NT], F32)
    negm = bump("negm", [128, NT], F32)
    rstd = bump("rstd", [128, NT], F32)
    st1 = bump("st1", [128, NT], F32)
    st2 = bump("st2", [128, NT], F32)
    c_am = bump("c_am", [128, 512], F32)
    c_t = bump("c_t", [128, 512], F32)
    c_i = bump("c_i", [128, 512], I32)
    c_f = bump("c_f", [128, 512], F32)
    c_s = bump("c_s", [128, 512], F32)
    c_si = bump("c_si", [128, 512], F32)
    c_bf = bump("c_bf", [128, 512], BF16)
    m0_t = bump("m0_t", [128, 128], BF16)
    m1_t = bump("m1_t", [128, 128], BF16)
    onc = bump("onc", [128, 1], BF16)
    onr = bump("onr", [1, 128], F32)
    o11 = bump("o11", [1, 1], F32)
    kon = bump("kon", [128, 8], F32)
    K_TINY, K_EPS, K_NCSH, K_LN2, K_NLN2, K_NHALF, K_ASC = (
        kon[:, 0:1], kon[:, 1:2], kon[:, 2:3], kon[:, 3:4], kon[:, 4:5],
        kon[:, 5:6], kon[:, 6:7])
    bsum_bc = bump("bsum_bc", [128, D], F32)
    bfc_cols = bump("bfc_cols", [128, 32], F32)
    recips = bump("recips", [128, NOWN], F32)
    denom_r = bump("denom_r", [1, 512], F32)
    xpb = bump("xpb", [128, D], F32)
    sdum = bump("sdum", [128, 1], F32)
    acc = bump("acc", [128, NOWN * D], BF16)
    PERSIST_END = ptr[0]

    wqkvT = bump("wqkvT", [128, 8 * 1536], BF16)
    woutT = bump("woutT", [128, 4 * D], BF16)
    STAGE0 = ptr[0]
    kfm = bump("kfm", [128, 4 * S], BF16)
    qfm = bump("qfm", [128, 4 * 2048], BF16)
    vtm = bump("vtm", [128, NT * DH], BF16)
    P01_BASE = ptr[0]
    KB = 1024

    def at_off(name, off_kb, shape, dtype):
        return nc.alloc_sbuf_tensor_at(name, list(shape), dtype,
                                       offset=BASE0 + P01_BASE + int(off_kb * KB)).ap()

    xstream = at_off("xstream", 0, [128, 2 * D], F32)
    sqdump = at_off("sqdump", 8, [128, D], F32)
    xnbf = at_off("xnbf", 8, [128, 2 * D], BF16)
    xnfm = at_off("xnfm", 12, [128, 8 * 4 * 128], BF16)
    brow2 = at_off("brow2", 12, [1, 2 * D], F32)
    xcload = at_off("xcload", 0, [128, D], F32)
    xn32 = at_off("xn32", 4, [128, D], F32)
    xqbf = at_off("xqbf", 8, [128, D], BF16)
    xqst = at_off("xqst", 10, [128, 2 * 8 * 128], BF16)
    expP = at_off("expP", 14, [128, 2 * 512], BF16)
    avun = at_off("avun", 16, [128, 4 * 512], BF16)
    wraw = at_off("wraw", 20, [128, 2 * 1024], F32)
    wqbf = at_off("wqbf", 28, [128, 2 * 1024], BF16)
    wtst = at_off("wtst", 32, [128, 1024], BF16)
    vfmb = at_off("vfmb", 34, [128, 4 * 256], BF16)
    A_END = ptr[0] + 36 * KB
    ptr[0] = A_END
    P2_END = A_END

    ptr[0] = PERSIST_END
    h32 = bump("h32", [128, 32 * 512], F32)
    hq = bump("hq", [128, 32 * 512], BF16)
    xqfm = bump("xqfm", [128, 8 * 2048], BF16)
    wfc_l = bump("wfc_l", [128, 2 * 8 * 128], BF16)
    wpq_l = bump("wpq_l", [128, 2 * 1024], BF16)
    hqi = bump("hqi", [128, 2 * 512], I32)
    out_sb = bump("out_sb", [128, 2 * D], F32)
    B_END = ptr[0]

    PREP_END = A_END

    BASE = 16576
    LIMIT = nc.SBUF_PARTITION_SIZE_BYTES
    assert max(A_END, P2_END, B_END, PREP_END) + BASE <= LIMIT - 32, (
        A_END, P2_END, B_END, PREP_END, LIMIT)

    psb = [nc.alloc_psum_tensor(f"psb{k}", [128, 512], F32).ap() for k in range(8)]

    p = Prog()
    SD, SG, SC2, SV, SS, ST, SOUT, SP = "sD", "sG", "sC", "sV", "sS", "sT", "sO", "sP"

    def vop(fn, wait=()):
        p.op("vector", fn, wait=wait, inc=(SV, 1))

    def sop(fn, wait=()):
        p.op("scalar", fn, wait=wait, inc=(SS, 1))

    def top(fn, wait=()):
        p.op("tensor", fn, wait=wait, inc=(ST, 1))

    pa_v = {0: 0, 1: 0}
    pa_d = {0: 0, 1: 0}
    for j in range(12):
        sl = j % 2
        p.dma("sync", SD, wraw[:, sl * 1024:(sl + 1) * 1024],
              wqkv_d[j * 128:(j + 1) * 128, :], wait=[(SV, pa_v[sl])])
        dld = p.v(SD)
        vop(lambda e, s=sl: e.tensor_copy(out=wqbf[:, s * 1024:(s + 1) * 1024],
                                          in_=wraw[:, s * 1024:(s + 1) * 1024]),
            wait=[(SD, dld), (SD, pa_d[sl])])
        pa_v[sl] = p.v(SV)
        p.dma("sync", SD,
              wqkvT.rearrange("q (d c) -> q d c", c=1536)[:, :, j * 128:(j + 1) * 128],
              wqbf[:, sl * 1024:(sl + 1) * 1024].rearrange("q (d c) -> q d c", c=128),
              wait=[(SV, pa_v[sl])], transpose=True)
        pa_d[sl] = p.v(SD)
    wq_tr = p.v(SD)
    for jj in range(4):
        sl = jj % 2
        p.dma("sync", SD, wraw[:, sl * 1024:(sl + 1) * 1024].rearrange("q (a c) -> q a c", c=512),
              wout_d.rearrange("(j q) c -> q j c", q=128)[:, 2 * jj:2 * jj + 2, :],
              wait=[(SV, pa_v[sl])])
        dld = p.v(SD)
        vop(lambda e, s=sl: e.tensor_copy(out=wqbf[:, s * 1024:(s + 1) * 1024],
                                          in_=wraw[:, s * 1024:(s + 1) * 1024]),
            wait=[(SD, dld), (SD, pa_d[sl])])
        pa_v[sl] = p.v(SV)
        for k2 in range(2):
            j = 2 * jj + k2
            p.dma("sync", SD,
                  woutT.rearrange("q (d c) -> q d c", c=1024)[:, :, j * 128:(j + 1) * 128],
                  wqbf[:, sl * 1024 + k2 * 512:sl * 1024 + (k2 + 1) * 512].rearrange("q (d c) -> q d c", c=128),
                  wait=[(SV, pa_v[sl])] if k2 == 0 else [], transpose=True)
        pa_d[sl] = p.v(SD)
    wo_tr = p.v(SD)
    p.dma("sync", SD, brow2[0:1, 0:D], bout_d)
    p.dma("sync", SD, brow2[0:1, D:2 * D], bpj_d)
    p.dma("sync", SD, m0_t, m0_d)
    p.dma("sync", SD, m1_t, m1_d)
    p.dma("sync", SD, onc, onc_d)
    p.dma("sync", SD, onr, onr_d)
    p.dma("sync", SD, o11, o11_d)
    p.dma("sync", SD, kon, kon_d)
    p.dma("sync", SD, bfc_cols, bfc_d.rearrange("o (q t) -> (o q) t", q=128))
    c_ld = p.v(SD)
    vop(lambda e: e.tensor_tensor(out=brow2[0:1, 0:D], in0=brow2[0:1, 0:D],
                                  in1=brow2[0:1, D:2 * D], op=ALU.add), wait=[(SD, c_ld)])
    bs_v = p.v(SV)
    for h in range(2):
        top(lambda e, hh=h: e.matmul(psb[6 + hh], onr, brow2[0:1, hh * 512:(hh + 1) * 512],
                                     start=True, stop=True), wait=[(SV, bs_v)])
    bs_t = p.v(ST)
    for h in range(2):
        vop(lambda e, hh=h: e.tensor_copy(out=bsum_bc[:, hh * 512:(hh + 1) * 512],
                                          in_=psb[6 + hh]), wait=[(ST, bs_t)] if h == 0 else [])
    PREPA = {"v": p.v(SV), "d": p.v(SD), "t": p.v(ST), "s": p.v(SS)}

    chain_g = [0]

    def sw():
        return [(SV, p.v(SV))]

    def scale_chain(nb):
        sop(lambda e, n=nb: e.activation(out=c_t[:, 0:n], in_=c_am[:, 0:n], func=AF.Ln,
                                         bias=K_TINY), wait=[(SV, p.v(SV))])
        sv = p.v(SS)
        vop(lambda e, n=nb: e.tensor_scalar(out=c_t[:, 0:n], in0=c_t[:, 0:n],
                                            scalar1=1.0 / LN2, scalar2=-3.0,
                                            op0=ALU.mult, op1=ALU.add), wait=[(SS, sv)])
        vop(lambda e, n=nb: e.tensor_copy(out=c_i[:, 0:n], in_=c_t[:, 0:n]), wait=sw())
        vop(lambda e, n=nb: e.tensor_copy(out=c_f[:, 0:n], in_=c_i[:, 0:n]), wait=sw())
        vop(lambda e, n=nb: e.tensor_tensor(out=c_am[:, 0:n], in0=c_f[:, 0:n],
                                            in1=c_t[:, 0:n], op=ALU.is_gt), wait=sw())
        vop(lambda e, n=nb: e.tensor_tensor(out=c_f[:, 0:n], in0=c_f[:, 0:n],
                                            in1=c_am[:, 0:n], op=ALU.subtract), wait=sw())
        vv = p.v(SV)
        sop(lambda e, n=nb: e.activation(out=c_s[:, 0:n], in_=c_f[:, 0:n], func=AF.Exp,
                                         scale=K_LN2), wait=[(SV, vv)])
        sop(lambda e, n=nb: e.activation(out=c_si[:, 0:n], in_=c_f[:, 0:n], func=AF.Exp,
                                         scale=K_NLN2))
        sv = p.v(SS)
        vop(lambda e, n=nb: e.tensor_copy(out=c_bf[:, 0:n], in_=c_s[:, 0:n]), wait=[(SS, sv)])
        vop(lambda e, n=nb: e.tensor_copy(out=c_s[:, 0:n], in_=c_bf[:, 0:n]), wait=sw())
        vop(lambda e, n=nb: e.tensor_copy(out=c_bf[:, 0:n], in_=c_si[:, 0:n]), wait=sw())
        vop(lambda e, n=nb: e.tensor_copy(out=c_si[:, 0:n], in_=c_bf[:, 0:n]), wait=sw())
        chain_g[0] = p.v(SS)

    def quant_piece(raw_sl, out_sl):
        scale_chain(32)
        for hf in range(2):
            vop(lambda e, HF=hf, s=raw_sl: e.tensor_tensor(
                out=c_i[:, 0:512].rearrange("q (b w) -> q b w", w=32),
                in0=wraw[:, s * 1024 + HF * 512:s * 1024 + (HF + 1) * 512].rearrange("q (b w) -> q b w", w=32),
                in1=c_si[:, HF * 16:(HF + 1) * 16].unsqueeze(2).broadcast_to([128, 16, 32]),
                op=ALU.mult), wait=sw())
            vop(lambda e, HF=hf, s=out_sl: e.tensor_tensor(
                out=wqbf[:, s * 1024 + HF * 512:s * 1024 + (HF + 1) * 512].rearrange("q (b w) -> q b w", w=32),
                in0=c_i[:, 0:512].rearrange("q (b w) -> q b w", w=32),
                in1=c_s[:, HF * 16:(HF + 1) * 16].unsqueeze(2).broadcast_to([128, 16, 32]),
                op=ALU.mult), wait=sw())

    def prep_b_gen():
        g_raw = {0: 0, 1: 0}
        g_bf_sg = {0: 0, 1: 0}
        g_bf_sd = {0: 0, 1: 0}
        g_tr = 0
        first = True

        def piece_src(idx):
            if idx < 32:
                return wfc_d[idx * 128:(idx + 1) * 128, :]
            j, q4 = divmod(idx - 32, 4)
            return wpj_d[j * 128:(j + 1) * 128, q4 * 1024:(q4 + 1) * 1024]

        for grp in range(4):
            base = grp * 16
            for jj in range(16):
                idx = base + jj
                sl = jj % 2
                w = [(SV, g_raw[sl])]
                if first:
                    w += [(SV, PREPA["v"]), (SD, PREPA["d"])]
                first = False
                p.dma("gpsimd", SG, wraw[:, sl * 1024:(sl + 1) * 1024], piece_src(idx),
                      wait=w)
                dld = p.v(SG)
                vop(lambda e, s=sl, JJ=jj: e.tensor_reduce(
                    out=c_am[:, 32 * JJ:32 * JJ + 32],
                    in_=wraw[:, s * 1024:(s + 1) * 1024].rearrange("q (b w) -> q b w", w=32),
                    axis=AX.X, op=ALU.max, apply_absolute_value=True),
                    wait=[(SG, dld)] + ([(SS, chain_g[0])] if jj == 0 else []))
                g_raw[sl] = p.v(SV)
                yield
            scale_chain(512)
            yield
            for jj in range(16):
                idx = base + jj
                sl = jj % 2
                p.dma("gpsimd", SG, wraw[:, sl * 1024:(sl + 1) * 1024], piece_src(idx),
                      wait=[(SV, g_raw[sl])])
                dld = p.v(SG)
                for hf in range(2):
                    vop(lambda e, s=sl, HF=hf, JJ=jj: e.tensor_tensor(
                        out=c_i[:, 0:512].rearrange("q (b w) -> q b w", w=32),
                        in0=wraw[:, s * 1024 + HF * 512:s * 1024 + (HF + 1) * 512].rearrange("q (b w) -> q b w", w=32),
                        in1=c_si[:, 32 * JJ + HF * 16:32 * JJ + (HF + 1) * 16].unsqueeze(2).broadcast_to([128, 16, 32]),
                        op=ALU.mult),
                        wait=([(SG, dld)] if hf == 0 else []) + sw() +
                             ([(SG, g_bf_sg[sl]), (SD, g_bf_sd[sl])] if hf == 0 else []))
                    vop(lambda e, s=sl, HF=hf, JJ=jj: e.tensor_tensor(
                        out=wqbf[:, s * 1024 + HF * 512:s * 1024 + (HF + 1) * 512].rearrange("q (b w) -> q b w", w=32),
                        in0=c_i[:, 0:512].rearrange("q (b w) -> q b w", w=32),
                        in1=c_s[:, 32 * JJ + HF * 16:32 * JJ + (HF + 1) * 16].unsqueeze(2).broadcast_to([128, 16, 32]),
                        op=ALU.mult), wait=sw())
                g_raw[sl] = p.v(SV)
                qv = p.v(SV)
                if idx < 32:
                    p.dma("gpsimd", SG, wfcq_s[idx * 128:(idx + 1) * 128, :],
                          wqbf[:, sl * 1024:(sl + 1) * 1024], wait=[(SV, qv)])
                    g_bf_sg[sl] = p.v(SG)
                else:
                    j, q4 = divmod(idx - 32, 4)
                    p.dma("sync", SD, wtst.rearrange("q (b c) -> q b c", c=128),
                          wqbf[:, sl * 1024:(sl + 1) * 1024].rearrange("q (b c) -> q b c", c=128),
                          wait=[(SV, qv), (SG, g_tr)], transpose=True)
                    g_bf_sd[sl] = p.v(SD)
                    p.dma("gpsimd", SG,
                          wpqT_s.rearrange("(b q) c -> q b c", q=128)[:, 8 * q4:8 * q4 + 8, j * 128:(j + 1) * 128],
                          wtst.rearrange("q (b c) -> q b c", c=128),
                          wait=[(SD, g_bf_sd[sl])])
                    g_tr = p.v(SG)
                yield

    PB = prep_b_gen()
    pb_cnt = [0]

    def pb(n=1):
        for _ in range(n):
            if next(PB, -1) != -1:
                pb_cnt[0] += 1

    def pb_align():
        while pb_cnt[0] % 33 != 0:
            pb(1)

    xsA = {}
    for t in range(NT):
        sl = t % 2
        w = [(SV, xsA.get(("v", sl), 0)), (SS, xsA.get(("s", sl), 0))]
        p.dma("sync", SD, xstream[:, sl * D:(sl + 1) * D],
              xf_d[t * 128:(t + 1) * 128, :], wait=w)
        xld = p.v(SD)
        vop(lambda e, s=sl, tt=t: e.reduce_sum(
            out=ssum[:, tt:tt + 1], in_=xstream[:, s * D:(s + 1) * D], axis=AX.X),
            wait=[(SD, xld)])
        xsA[("v", sl)] = p.v(SV)
        sop(lambda e, s=sl, tt=t: e.activation(
            out=sqdump, in_=xstream[:, s * D:(s + 1) * D], func=AF.Square,
            accum_out=ssq[:, tt:tt + 1]), wait=[(SD, xld)])
        sop(lambda e: e.activation(out=sdum, in_=kon[:, 7:8], func=AF.Square))
        xsA[("s", sl)] = p.v(SS)
        pb(1)
    sv = p.v(SS)
    vop(lambda e: e.tensor_scalar(out=st1, in0=ssq, scalar1=1.0 / D,
                                  scalar2=None, op0=ALU.mult), wait=[(SS, sv)] + sw())
    vop(lambda e: e.tensor_scalar(out=negm, in0=ssum, scalar1=-1.0 / D,
                                  scalar2=None, op0=ALU.mult), wait=sw())
    vop(lambda e: e.tensor_tensor(out=st2, in0=negm, in1=negm, op=ALU.mult), wait=sw())
    vop(lambda e: e.tensor_tensor(out=st1, in0=st1, in1=st2, op=ALU.subtract), wait=sw())
    vv = p.v(SV)
    sop(lambda e: e.activation(out=st2, in_=st1, func=AF.Ln, bias=K_EPS), wait=[(SV, vv)])
    sop(lambda e: e.activation(out=rstd, in_=st2, func=AF.Exp, scale=K_NHALF))
    STATS_S = p.v(SS)

    xsB = {}
    xnbf_g = {0: 0, 1: 0}
    xnfm_g = {0: 0, 1: 0, 2: 0, 3: 0}
    bank_g = {}
    vtr_g = [0]
    p0_tr = {}

    def emit_pB_chunk(c):
        for k in range(2):
            t = 2 * c + k
            sl = t % 2
            w = [(SV, xsB.get(("v", sl), xsA.get(("v", sl), 0)))]
            if c == 0:
                w += [(SS, STATS_S)]
            p.dma("sync", SD, xstream[:, sl * D:(sl + 1) * D],
                  xf_d[t * 128:(t + 1) * 128, :], wait=w)
            xld = p.v(SD)
            wslot = t % 4
            vop(lambda e, s=sl, tt=t: e.tensor_scalar(
                out=xnbf[:, s * D:(s + 1) * D], in0=xstream[:, s * D:(s + 1) * D],
                scalar1=negm[:, tt:tt + 1], scalar2=rstd[:, tt:tt + 1],
                op0=ALU.add, op1=ALU.mult),
                wait=[(SD, xld), (SS, STATS_S), (SD, xnbf_g[sl])])
            xn_v = p.v(SV)
            wtr = [(SV, xn_v), (ST, xnfm_g[wslot])]
            if c == 0 and k == 0:
                wtr.append((ST, PREPA["t"]))
            p.dma("scalar", SC2,
                  xnfm.rearrange("q (d c) -> q d c", c=512)[:, :, wslot * 128:(wslot + 1) * 128],
                  xnbf[:, sl * D:(sl + 1) * D].rearrange("q (d c) -> q d c", c=128),
                  wait=wtr, transpose=True)
            xnbf_g[sl] = p.v(SC2)
            xsB[("v", sl)] = p.v(SV)
        p0_tr[c] = p.v(SC2)

    def emit_p1_chunk(c):
        s0 = (2 * c) % 4
        tr_done = p0_tr[c]
        for ec in range(4):
            bk = ec % 2
            for dmc in range(8):
                rr = xnfm[:, dmc * 512 + s0 * 128:dmc * 512 + s0 * 128 + 256]
                ll = wqkvT[:, dmc * 1536 + 512 + ec * 128:dmc * 1536 + 640 + ec * 128]
                w = [(SC2, tr_done), (SV, bank_g.get(bk, 0))] if dmc == 0 else []
                if dmc == 0 and ec == 0 and c == 0:
                    w.append((SD, wq_tr))
                top(lambda e, L=ll, R=rr, B=bk, M=dmc: e.matmul(
                    psb[B][:, 0:256], L, R, start=(M == 0), stop=(M == 7)), wait=w)
            mm = p.v(ST)
            vop(lambda e, E=ec, B=bk, C=c: e.tensor_copy(
                out=kfm[:, E * S + C * 256:E * S + (C + 1) * 256], in_=psb[B][:, 0:256]),
                wait=[(ST, mm)])
            bank_g[bk] = p.v(SV)
        for ec in range(4):
            bk = ec % 2
            for dmc in range(8):
                rr = xnfm[:, dmc * 512 + s0 * 128:dmc * 512 + s0 * 128 + 256]
                ll = wqkvT[:, dmc * 1536 + 1024 + ec * 128:dmc * 1536 + 1152 + ec * 128]
                w = [(SV, bank_g.get(bk, 0))] if dmc == 0 else []
                if dmc == 0 and ec == 0:
                    w.append((SC2, vtr_g[0]))
                top(lambda e, L=ll, R=rr, B=bk, M=dmc: e.matmul(
                    psb[B][:, 0:256], L, R, start=(M == 0), stop=(M == 7)), wait=w)
            mm = p.v(ST)
            vop(lambda e, E=ec, B=bk: e.tensor_copy(
                out=vfmb[:, E * 256:(E + 1) * 256], in_=psb[B][:, 0:256]), wait=[(ST, mm)])
            bank_g[bk] = p.v(SV)
        vv2 = p.v(SV)
        for ec in range(4):
            p.dma("scalar", SC2,
                  vtm.rearrange("q (a c) -> q a c", c=512)[:, 2 * c:2 * c + 2, ec * 128:(ec + 1) * 128],
                  vfmb[:, ec * 256:(ec + 1) * 256].rearrange("q (a c) -> q a c", c=128),
                  wait=[(SV, vv2)] if ec == 0 else [], transpose=True)
        vtr_g[0] = p.v(SC2)
        for ec in range(4):
            bk = 2 + ec % 2
            for dmc in range(8):
                rr = xnfm[:, dmc * 512 + s0 * 128:dmc * 512 + s0 * 128 + 128]
                ll = wqkvT[:, dmc * 1536 + ec * 128:dmc * 1536 + (ec + 1) * 128]
                w = [(SV, bank_g.get(bk, 0))] if dmc == 0 else []
                top(lambda e, L=ll, R=rr, B=bk, M=dmc: e.matmul(
                    psb[B][:, 0:128], L, R, start=(M == 0), stop=(M == 7)), wait=w)
            mm = p.v(ST)
            vop(lambda e, E=ec, B=bk, C=c: e.tensor_copy(
                out=qfm[:, E * 2048 + C * 128:E * 2048 + (C + 1) * 128], in_=psb[B][:, 0:128]),
                wait=[(ST, mm)])
            bank_g[bk] = p.v(SV)
        xnfm_g[s0] = p.v(ST)
        xnfm_g[(s0 + 1) % 4] = p.v(ST)

    for c in range(NCH):
        emit_pB_chunk(c)
        pb(2)
        if c >= 1:
            emit_p1_chunk(c - 1)
    emit_p1_chunk(NCH - 1)
    P1 = {"v": p.v(SV), "d": p.v(SD), "t": p.v(ST), "g": p.v(SG), "c": p.v(SC2)}

    pc_state = {}

    def pass_c_gen():
        xc_g = 0
        xq_g = {}
        for i in range(NOWN):
            t = 2 * i
            w = [(SV, xc_g)]
            if i == 0:
                w += [(SV, P1["v"]), (ST, P1["t"]), (SD, P1["d"])]
            p.dma("sync", SD, xcload, xo_d[i * 128:(i + 1) * 128, :], wait=w)
            xld = p.v(SD)
            vop(lambda e, tt=t: e.tensor_scalar(
                out=xn32, in0=xcload,
                scalar1=negm[:, tt:tt + 1], scalar2=rstd[:, tt:tt + 1],
                op0=ALU.add, op1=ALU.mult), wait=[(SD, xld)] + sw())
            xc_g = p.v(SV)
            vop(lambda e, I=i: e.tensor_reduce(
                out=c_am[:, 32 * I:32 * I + 32],
                in_=xn32.rearrange("q (b w) -> q b w", w=32),
                axis=AX.X, op=ALU.max, apply_absolute_value=True),
                wait=([(SS, chain_g[0])] if i == 0 else []))
            yield
        scale_chain(512)
        yield
        for i in range(NOWN):
            t = 2 * i
            ssl = i % 2
            p.dma("sync", SD, xcload, xo_d[i * 128:(i + 1) * 128, :], wait=[(SV, xc_g)])
            xld = p.v(SD)
            vop(lambda e, tt=t: e.tensor_scalar(
                out=xn32, in0=xcload,
                scalar1=negm[:, tt:tt + 1], scalar2=rstd[:, tt:tt + 1],
                op0=ALU.add, op1=ALU.mult), wait=[(SD, xld)] + sw())
            xc_g = p.v(SV)
            for hf in range(2):
                vop(lambda e, HF=hf, I=i: e.tensor_tensor(
                    out=c_i[:, 0:512].rearrange("q (b w) -> q b w", w=32),
                    in0=xn32[:, HF * 512:(HF + 1) * 512].rearrange("q (b w) -> q b w", w=32),
                    in1=c_si[:, 32 * I + HF * 16:32 * I + (HF + 1) * 16].unsqueeze(2).broadcast_to([128, 16, 32]),
                    op=ALU.mult), wait=sw())
                vop(lambda e, HF=hf, I=i: e.tensor_tensor(
                    out=xqbf[:, HF * 512:(HF + 1) * 512].rearrange("q (b w) -> q b w", w=32),
                    in0=c_i[:, 0:512].rearrange("q (b w) -> q b w", w=32),
                    in1=c_s[:, 32 * I + HF * 16:32 * I + (HF + 1) * 16].unsqueeze(2).broadcast_to([128, 16, 32]),
                    op=ALU.mult),
                    wait=[(SD, xq_g.get(("tr", 1 - ssl), 0))] if hf == 0 else [])
            xq_v = p.v(SV)
            p.dma("sync", SD,
                  xqst[:, ssl * 1024:(ssl + 1) * 1024].rearrange("q (d c) -> q d c", c=128),
                  xqbf.rearrange("q (d c) -> q d c", c=128),
                  wait=[(SV, xq_v), (SG, xq_g.get(("sp", ssl), 0))], transpose=True)
            xq_g[("tr", ssl)] = p.v(SD)
            p.dma("gpsimd", SG,
                  xq_s.rearrange("q (d c) -> q d c", c=2048)[:, :, i * 128:(i + 1) * 128],
                  xqst[:, ssl * 1024:(ssl + 1) * 1024].rearrange("q (d c) -> q d c", c=128),
                  wait=[(SD, xq_g[("tr", ssl)])])
            xq_g[("sp", ssl)] = p.v(SG)
            yield
        pc_state["done"] = True

    PC = pass_c_gen()
    pc_state["done"] = False

    pb_align()
    sc_rd = {4: 0, 5: 0}
    expP_rd_v = {0: 0, 1: 0}
    expP_rd_t = {0: 0, 1: 0}
    av_prev_ev = 0
    prev_acc_v = 0
    denom_rd_t = 0
    rc_v = 0
    xpb_v_free = 0

    for g in range(4):
        ktmax = 8 * g + 8
        gsl = g % 2

        def emit_scores(kt, g=g, gsl=gsl):
            cstart = max(0, kt // 2 - 4 * g) * 128
            cols = 512 - cstart
            qcol0 = g * 512 + cstart
            sb = 4 + (kt % 2)
            for dc in range(4):
                w = []
                if dc == 0:
                    w = [(SS, sc_rd[sb])]
                    if kt == 0 and g == 0:
                        w += [(SV, P1["v"]), (SD, P1["d"]), (SC2, P1["c"])]
                ll = kfm[:, dc * S + kt * 128:dc * S + (kt + 1) * 128]
                rr = qfm[:, dc * 2048 + qcol0:dc * 2048 + qcol0 + cols]
                top(lambda e, L=ll, R=rr, B=sb, DC=dc, CO=cols: e.matmul(
                    psb[B][:, 0:CO], L, R, start=(DC == 0), stop=(DC == 3)), wait=w)
            return p.v(ST), cols, cstart

        sc_done, cols, cstart = emit_scores(0)
        for kt in range(ktmax):
            esl = kt % 2
            sb = 4 + esl
            sop(lambda e, B=sb, CO=cols, E=esl: e.activation(
                out=expP[:, E * 512:E * 512 + CO], in_=psb[B][:, 0:CO],
                func=AF.Exp, bias=K_NCSH, scale=K_ASC),
                wait=[(ST, sc_done), (SP, expP_rd_v[esl]), (ST, expP_rd_t[esl])])
            sc_rd[sb] = p.v(SS)
            exp_done = p.v(SS)
            wv = [(SS, exp_done)]
            if kt >= 8 * g:
                mt = m0_t if kt % 2 == 0 else m1_t
                p.op("gpsimd", lambda e, E=esl, M=mt: e.tensor_tensor(
                    out=expP[:, E * 512:E * 512 + 128], in0=expP[:, E * 512:E * 512 + 128],
                    in1=M, op=ALU.mult), wait=[(SS, exp_done)], inc=(SP, 1))
                wv.append((SP, p.v(SP)))
            expP_rd_v[esl] = p.v(SP)
            cs_cur, co_cur = cstart, cols
            if kt + 1 < ktmax:
                sc_done, cols, cstart = emit_scores(kt + 1)
            for dc in range(4):
                ll = vtm[:, kt * 512 + dc * 128:kt * 512 + (dc + 1) * 128]
                w = list(wv) if dc == 0 else []
                if dc == 0 and kt == 0:
                    w += [(SV, prev_acc_v), (SV, av_prev_ev)]
                top(lambda e, L=ll, DC=dc, CS=cs_cur, CO=co_cur, E=esl, F=(kt == 0),
                    LAST=(kt == ktmax - 1): e.matmul(
                    psb[DC][:, CS:CS + CO], L, expP[:, E * 512:E * 512 + CO],
                    start=F, stop=LAST, skip_group_check=True), wait=w)
            top(lambda e, CS=cs_cur, CO=co_cur, E=esl, F=(kt == 0),
                LAST=(kt == ktmax - 1): e.matmul(
                psb[6][0:1, CS:CS + CO], onc, expP[:, E * 512:E * 512 + CO],
                start=F, stop=LAST, skip_group_check=True))
            expP_rd_t[esl] = p.v(ST)
            if not pc_state["done"]:
                next(PC, None)
            else:
                pb(2)
        av_mm = p.v(ST)
        vop(lambda e: e.tensor_copy(out=denom_r, in_=psb[6][0:1, 0:512]),
            wait=[(ST, av_mm), (ST, denom_rd_t)])
        for dc in range(4):
            w = [(ST, av_mm)] if dc == 0 else []
            vop(lambda e, DC=dc: e.tensor_copy(
                out=avun[:, DC * 512:(DC + 1) * 512], in_=psb[DC]), wait=w)
        av_prev_ev = p.v(SV)
        dn_v = p.v(SV)
        for ti in range(4):
            i = 4 * g + ti
            top(lambda e, TI=ti: e.matmul(psb[7][:, 0:1],
                                          denom_r[0:1, TI * 128:(TI + 1) * 128], o11,
                                          start=True, stop=True),
                wait=[(SV, dn_v), (SV, rc_v)])
            vop(lambda e, I=i: e.reciprocal(out=recips[:, I:I + 1], in_=psb[7][:, 0:1]),
                wait=[(ST, p.v(ST))])
            rc_v = p.v(SV)
            p.dma("sync", SD, xpb, xo_d[i * 128:(i + 1) * 128, :], wait=[(SV, xpb_v_free)])
            vop(lambda e: e.tensor_tensor(out=xpb, in0=xpb, in1=bsum_bc, op=ALU.add),
                wait=[(SD, p.v(SD))])
            xpb_v = p.v(SV)
            for dc in range(4):
                for oh in range(2):
                    w = []
                    if dc == 0 and oh == 0:
                        w = [(SV, av_prev_ev), (SV, prev_acc_v)]
                        if g == 0 and ti == 0:
                            w.append((SD, wo_tr))
                    ll = avun[:, dc * 512 + ti * 128:dc * 512 + (ti + 1) * 128]
                    rr2 = woutT[:, dc * D + oh * 512:dc * D + (oh + 1) * 512]
                    top(lambda e, L=ll, R=rr2, OH=oh, DC=dc: e.matmul(
                        psb[OH], L, R, start=(DC == 0), stop=(DC == 3)), wait=w)
            opmm = p.v(ST)
            for oh in range(2):
                vop(lambda e, OH=oh, I=i: e.scalar_tensor_tensor(
                    out=acc[:, I * D + OH * 512:I * D + (OH + 1) * 512],
                    in0=psb[OH], scalar=recips[:, I:I + 1],
                    in1=xpb[:, OH * 512:(OH + 1) * 512],
                    op0=ALU.mult, op1=ALU.add),
                    wait=([(ST, opmm), (SV, xpb_v)] + sw()) if oh == 0 else [])
            prev_acc_v = p.v(SV)
            xpb_v_free = p.v(SV)
        denom_rd_t = p.v(ST)
    for _ in PC:
        pass
    for _ in PB:
        pass
    P2 = {"v": p.v(SV), "s": p.v(SS), "t": p.v(ST), "d": p.v(SD), "g": p.v(SG)}
    if dbg:
        p.dma("gpsimd", SG, dbg_acc, acc, wait=[(SV, P2["v"])])
        p.dma("gpsimd", SG, dbg_rec, recips)

    p.dma("sync", SD, xqfm, xq_s, wait=[(SV, P2["v"]), (ST, P2["t"]), (SS, P2["s"]),
                                        (SG, P2["g"])])
    xq_ld = p.v(SD)
    wfc_rows = wfcq_s.rearrange("(q s) c -> s q c", s=32)
    wpq_rows = wpqT_s.rearrange("(q s) c -> s q c", s=32)
    wfc_g = {0: 0, 1: 0}
    wpq_g = {0: 0, 1: 0}
    fc_rd = {0: 0, 1: 0}
    pj_guard = 0
    h32_rd_prev = 0
    hq_rd_prev_t = 0
    out_sb_g = {0: 0, 1: 0}
    for sp in range(4):
        for hc in range(32):
            sl = hc % 2
            w = [(ST, wfc_g[sl])]
            if sp == 0 and hc == 0:
                w += [(SD, xq_ld), (ST, P2["t"]), (SG, P2["g"])]
            p.dma("sync", SD,
                  wfc_l[:, sl * 1024:(sl + 1) * 1024].rearrange("q (b c) -> q b c", c=128),
                  wfc_rows[hc].rearrange("q (b c) -> q b c", c=128),
                  wait=w, transpose=True)
            trd = p.v(SD)
            bk = hc % 2
            for dmc in range(8):
                w = []
                if dmc == 0:
                    w = [(SD, trd), (SS, fc_rd[bk])]
                    if sp == 0 and hc < 2:
                        w += [(SV, P2["v"])]
                ll = wfc_l[:, sl * 1024 + dmc * 128:sl * 1024 + (dmc + 1) * 128]
                rr = xqfm[:, dmc * 2048 + sp * 512:dmc * 2048 + (sp + 1) * 512]
                top(lambda e, L=ll, R=rr, B=bk, M=dmc: e.matmul(
                    psb[B], L, R, start=(M == 0), stop=(M == 7)), wait=w)
            wfc_g[sl] = p.v(ST)
            mmv = p.v(ST)
            w = [(ST, mmv)]
            if hc == 0:
                w.append((SV, h32_rd_prev))
            sop(lambda e, HC=hc, B=bk: e.activation(
                out=h32[:, HC * 512:(HC + 1) * 512], in_=psb[B],
                func=AF.Gelu, bias=bfc_cols[:, HC:HC + 1]), wait=w)
            fc_rd[bk] = p.v(SS)
            w = [(SS, fc_rd[bk])]
            if hc == 0:
                w.append((SS, chain_g[0]))
            vop(lambda e, HC=hc, FIRST=(hc == 0): e.tensor_tensor(
                out=c_am[:, 0:512],
                in0=(h32[:, HC * 512:(HC + 1) * 512] if FIRST else c_am[:, 0:512]),
                in1=h32[:, HC * 512:(HC + 1) * 512], op=ALU.max), wait=w)
            vop(lambda e, HC=hc, FIRST=(hc == 0): e.tensor_tensor(
                out=c_t[:, 0:512],
                in0=(h32[:, HC * 512:(HC + 1) * 512] if FIRST else c_t[:, 0:512]),
                in1=h32[:, HC * 512:(HC + 1) * 512], op=ALU.min))
        vop(lambda e: e.tensor_scalar(out=c_t[:, 0:512], in0=c_t[:, 0:512],
                                      scalar1=-1.0, scalar2=None, op0=ALU.mult),
            wait=sw())
        vop(lambda e: e.tensor_tensor(out=c_am[:, 0:512], in0=c_am[:, 0:512],
                                      in1=c_t[:, 0:512], op=ALU.max), wait=sw())
        scale_chain(512)
        hq_done = {}
        for hc in range(32):
            qsl = hc % 2
            w = sw()
            if hc == 0:
                w += [(ST, hq_rd_prev_t)]
            vop(lambda e, HC=hc, QS=qsl: e.tensor_tensor(
                out=hqi[:, QS * 512:(QS + 1) * 512], in0=h32[:, HC * 512:(HC + 1) * 512],
                in1=c_si[:, 0:512], op=ALU.mult), wait=w)
            vop(lambda e, HC=hc, QS=qsl: e.tensor_tensor(
                out=hq[:, HC * 512:(HC + 1) * 512], in0=hqi[:, QS * 512:(QS + 1) * 512],
                in1=c_s[:, 0:512], op=ALU.mult))
            hq_done[hc] = p.v(SV)
        h32_rd_prev = p.v(SV)
        for pair in range(2):
            for hc in range(32):
                sl = hc % 2
                w = [(ST, wpq_g[sl])]
                if sp == 0 and pair == 0 and hc < 2:
                    w += [(ST, P2["t"]), (SG, P2["g"])]
                p.dma("scalar", SC2, wpq_l[:, sl * 1024:(sl + 1) * 1024], wpq_rows[hc],
                      wait=w)
                wld = p.v(SC2)
                for ti in range(2):
                    ilocal = pair * 2 + ti
                    for oh in range(2):
                        w = []
                        if ti == 0 and oh == 0:
                            w = [(SC2, wld), (SV, hq_done[hc if pair == 0 else 31])]
                            if hc == 0:
                                w += [(SV, pj_guard)]
                        ll = hq[:, hc * 512 + ilocal * 128:hc * 512 + (ilocal + 1) * 128]
                        rr2 = wpq_l[:, sl * 1024 + oh * 512:sl * 1024 + (oh + 1) * 512]
                        top(lambda e, L=ll, R=rr2, TI=ti, OH=oh, HC=hc: e.matmul(
                            psb[2 + TI * 2 + OH], L, R, start=(HC == 0), stop=(HC == 31)),
                            wait=w)
                wpq_g[sl] = p.v(ST)
            pjmm = p.v(ST)
            for ti in range(2):
                i = 4 * sp + pair * 2 + ti
                osl = ti
                for oh in range(2):
                    w = []
                    if oh == 0:
                        w = [(ST, pjmm), (SOUT, out_sb_g[osl])]
                    vop(lambda e, TI=ti, OH=oh, I=i, OS=osl: e.tensor_tensor(
                        out=out_sb[:, OS * D + OH * 512:OS * D + (OH + 1) * 512],
                        in0=psb[2 + TI * 2 + OH],
                        in1=acc[:, I * D + OH * 512:I * D + (OH + 1) * 512],
                        op=ALU.add), wait=w)
                ev = p.v(SV)
                p.dma("gpsimd", SOUT, y_d[i * 128:(i + 1) * 128, :],
                      out_sb[:, osl * D:(osl + 1) * D], wait=[(SV, ev)])
                out_sb_g[osl] = p.v(SOUT)
            pj_guard = p.v(SV)
        hq_rd_prev_t = p.v(ST)

    if dbg:
        p.dma("gpsimd", SG, dbg_h32, h32, wait=[(SV, p.v(SV)), (SS, p.v(SS))])
        p.dma("gpsimd", SG, dbg_hq, hq)
        p.op("gpsimd", lambda e: e.engine_nop(), wait=[(SG, p.v(SG))])
    p.op("gpsimd", lambda e: e.engine_nop(), wait=[(SOUT, p.v(SOUT))])

    from contextlib import ExitStack
    with ExitStack() as ctx:
        sems = {}
        for s in [SD, SG, SC2, SV, SS, ST, SOUT, SP]:
            sems[s] = ctx.enter_context(nc.semaphore(s))
        block = ctx.enter_context(nc.Block())

        @block.sync
        def _(eng):
            run_stream(p, "sync", eng, sems)

        @block.vector
        def _(eng):
            run_stream(p, "vector", eng, sems)

        @block.scalar
        def _(eng):
            run_stream(p, "scalar", eng, sems)

        @block.tensor
        def _(eng):
            run_stream(p, "tensor", eng, sems)

        @block.gpsimd
        def _(eng):
            run_stream(p, "gpsimd", eng, sems)
    return nc


_NC_CACHE = [None]


def kernel(L0_attn_x, L0_attn_w_qkv, L0_attn_w_out, L0_attn_b_out,
           L0_ffn_w_fc, L0_ffn_b_fc, L0_ffn_w_proj, L0_ffn_b_proj):
    x = np.ascontiguousarray(np.asarray(L0_attn_x, dtype=np.float32))
    wqkv = np.ascontiguousarray(np.asarray(L0_attn_w_qkv, dtype=np.float32))
    wout = np.ascontiguousarray(np.asarray(L0_attn_w_out, dtype=np.float32))
    bout = np.ascontiguousarray(np.asarray(L0_attn_b_out, dtype=np.float32)).reshape(1, D)
    wfc = np.ascontiguousarray(np.asarray(L0_ffn_w_fc, dtype=np.float32))
    bfc = np.ascontiguousarray(np.asarray(L0_ffn_b_fc, dtype=np.float32)).reshape(1, HID)
    wpj = np.ascontiguousarray(np.asarray(L0_ffn_w_proj, dtype=np.float32))
    bpj = np.ascontiguousarray(np.asarray(L0_ffn_b_proj, dtype=np.float32)).reshape(1, D)

    if _NC_CACHE[0] is None:
        _NC_CACHE[0] = build()
    nc = _NC_CACHE[0]

    ii, jj = np.meshgrid(np.arange(128), np.arange(128), indexing="ij")
    tril_keep = (ii <= jj).astype(ml_dtypes.bfloat16)
    zeros = np.zeros((128, 128), ml_dtypes.bfloat16)
    ones = np.ones((128, 128), ml_dtypes.bfloat16)
    ones_col = np.ones((128, 1), ml_dtypes.bfloat16)
    ones_row = np.ones((1, 128), np.float32)
    one11 = np.ones((1, 1), np.float32)
    konst = np.tile(np.array([[TINY, EPS, -CSH, LN2, -LN2, -0.5, ATT_SC, 0.0]],
                             np.float32), (128, 1))

    swap = np.arange(NT).reshape(-1, 2)[:, ::-1].reshape(-1)
    in_maps = []
    for c in range(8):
        b, r = c // 2, c % 2
        xl = x[b].reshape(NT, 128, D)
        if r:
            xl = xl[swap]
        in_maps.append({
            "x_full": np.ascontiguousarray(xl.reshape(S, D)),
            "x_own": np.ascontiguousarray(xl[0::2].reshape(NOWN * 128, D)),
            "wqkv": wqkv, "wout": wout, "bout": bout,
            "wfc": wfc, "bfc": bfc, "wproj": wpj, "bproj": bpj,
            "m0": np.ascontiguousarray(tril_keep),
            "m1": np.ascontiguousarray(ones if r else zeros),
            "ones_col": ones_col, "ones_row": ones_row, "one11": one11,
            "konst": konst,
        })
    res = run_bass_kernel_spmd(nc, in_maps, core_ids=list(range(8)))
    out = np.empty((4, S, D), np.float32)
    for c, rr in enumerate(res.results):
        b, r = c // 2, c % 2
        y = rr["y"].reshape(NOWN, 128, D)
        ob = out[b].reshape(NT, 128, D)
        for i in range(NOWN):
            ob[2 * i + r] = y[i]
    return out


# version 10
# speedup vs baseline: 1.1940x; 1.1817x over previous
import numpy as np
import ml_dtypes
import concourse.bass as bass
import concourse.mybir as mybir
from concourse.bass_utils import run_bass_kernel_spmd

F32 = mybir.dt.float32
I32 = mybir.dt.int32
BF16 = mybir.dt.bfloat16
AF = mybir.ActivationFunctionType
ALU = mybir.AluOpType
AX = mybir.AxisListType

S = 4096
D = 1024
DH = 512
HID = 4096
NT = 32
NOWN = 16
NCH = 16
ATT_SC = float(DH) ** -0.5
CSH = 15.0
LN2 = float(np.log(2.0))
TINY = 1e-37
EPS = 1e-5


class Prog:
    def __init__(self):
        self.ops = []
        self.cnt = {}

    def v(self, s):
        return self.cnt.get(s, 0)

    def op(self, eng, fn, wait=(), inc=None):
        if inc is not None:
            s, n = inc
            self.cnt[s] = self.cnt.get(s, 0) + n
        self.ops.append((eng, fn, tuple(wait), inc))

    def dma(self, eng, sem, out, in_, wait=(), **kw):
        self.op(eng, lambda e, o=out, i=in_, k=kw: e.dma_start(out=o, in_=i, **k),
                wait=wait, inc=(sem, 16))


def run_stream(prog, name, eng, sems):
    for e, fn, waits, inc in prog.ops:
        if e != name:
            continue
        for s, v in waits:
            if v > 0:
                eng.wait_ge(sems[s], v)
        ins = fn(eng)
        if inc is not None:
            ins.then_inc(sems[inc[0]], inc[1])


def build(dbg=False):
    nc = bass.Bass("TRN2", target_bir_lowering=False)

    xf_d = nc.dram_tensor("x_full", [S, D], F32, kind="ExternalInput").ap()
    xo_d = nc.dram_tensor("x_own", [NOWN * 128, D], F32, kind="ExternalInput").ap()
    wqkv_d = nc.dram_tensor("wqkv", [1536, D], F32, kind="ExternalInput").ap()
    wout_d = nc.dram_tensor("wout", [D, DH], F32, kind="ExternalInput").ap()
    bout_d = nc.dram_tensor("bout", [1, D], F32, kind="ExternalInput").ap()
    wfc_d = nc.dram_tensor("wfc", [HID, D], F32, kind="ExternalInput").ap()
    bfc_d = nc.dram_tensor("bfc", [1, HID], F32, kind="ExternalInput").ap()
    wpj_d = nc.dram_tensor("wproj", [D, HID], F32, kind="ExternalInput").ap()
    bpj_d = nc.dram_tensor("bproj", [1, D], F32, kind="ExternalInput").ap()
    m0_d = nc.dram_tensor("m0", [128, 128], BF16, kind="ExternalInput").ap()
    m1_d = nc.dram_tensor("m1", [128, 128], BF16, kind="ExternalInput").ap()
    onc_d = nc.dram_tensor("ones_col", [128, 1], BF16, kind="ExternalInput").ap()
    onr_d = nc.dram_tensor("ones_row", [1, 128], F32, kind="ExternalInput").ap()
    o11_d = nc.dram_tensor("one11", [1, 1], F32, kind="ExternalInput").ap()
    kon_d = nc.dram_tensor("konst", [128, 8], F32, kind="ExternalInput").ap()
    y_d = nc.dram_tensor("y", [NOWN * 128, D], F32, kind="ExternalOutput").ap()
    SCR = "ExternalOutput" if dbg else "Internal"
    wfcq_s = nc.dram_tensor("wfcq_s", [HID, D], BF16, kind=SCR).ap()
    wpqT_s = nc.dram_tensor("wpqT_s", [HID, D], BF16, kind=SCR).ap()
    xq_s = nc.dram_tensor("xq_s", [128, 8 * 2048], BF16, kind=SCR).ap()
    if dbg:
        dbg_kfm = nc.dram_tensor("dbg_kfm", [128, 4 * S], BF16, kind="ExternalOutput").ap()
        dbg_qfm = nc.dram_tensor("dbg_qfm", [128, 4 * 2048], BF16, kind="ExternalOutput").ap()
        dbg_vtm = nc.dram_tensor("dbg_vtm", [128, NT * DH], BF16, kind="ExternalOutput").ap()
        dbg_acc = nc.dram_tensor("dbg_acc", [128, NOWN * D], BF16, kind="ExternalOutput").ap()
        dbg_rec = nc.dram_tensor("dbg_rec", [128, NOWN], F32, kind="ExternalOutput").ap()
        dbg_stats = nc.dram_tensor("dbg_stats", [128, 4 * NT], F32, kind="ExternalOutput").ap()
        dbg_h32 = nc.dram_tensor("dbg_h32", [128, 32 * 512], F32, kind="ExternalOutput").ap()
        dbg_hq = nc.dram_tensor("dbg_hq", [128, 32 * 512], BF16, kind="ExternalOutput").ap()

    SZ = {F32: 4, I32: 4, BF16: 2}
    ptr = [0]

    BASE0 = 16576

    def bump(name, shape, dtype, align=64):
        pp = (ptr[0] + align - 1) // align * align
        nb = int(np.prod(shape[1:])) * SZ[dtype] if len(shape) > 1 else SZ[dtype]
        ptr[0] = pp + nb
        return nc.alloc_sbuf_tensor_at(name, list(shape), dtype, offset=BASE0 + pp).ap()

    ssum = bump("ssum", [128, NT], F32)
    ssq = bump("ssq", [128, NT], F32)
    negm = bump("negm", [128, NT], F32)
    rstd = bump("rstd", [128, NT], F32)
    st1 = bump("st1", [128, NT], F32)
    st2 = bump("st2", [128, NT], F32)
    c_am = bump("c_am", [128, 512], F32)
    c_t = bump("c_t", [128, 512], F32)
    c_i = bump("c_i", [128, 512], I32)
    c_f = bump("c_f", [128, 512], F32)
    c_s = bump("c_s", [128, 512], F32)
    c_si = bump("c_si", [128, 512], F32)
    c_bf = bump("c_bf", [128, 512], BF16)
    m0_t = bump("m0_t", [128, 128], BF16)
    m1_t = bump("m1_t", [128, 128], BF16)
    onc = bump("onc", [128, 1], BF16)
    onr = bump("onr", [1, 128], F32)
    o11 = bump("o11", [1, 1], F32)
    kon = bump("kon", [128, 8], F32)
    K_TINY, K_EPS, K_NCSH, K_LN2, K_NLN2, K_NHALF, K_ASC = (
        kon[:, 0:1], kon[:, 1:2], kon[:, 2:3], kon[:, 3:4], kon[:, 4:5],
        kon[:, 5:6], kon[:, 6:7])
    bsum_bc = bump("bsum_bc", [128, D], F32)
    bfc_cols = bump("bfc_cols", [128, 32], F32)
    recips = bump("recips", [128, NOWN], F32)
    denom_r = bump("denom_r", [1, 512], F32)
    xpb = bump("xpb", [128, D], F32)
    sdum = bump("sdum", [128, 1], F32)
    acc = bump("acc", [128, NOWN * D], BF16)
    PERSIST_END = ptr[0]

    wqkvT = bump("wqkvT", [128, 8 * 1536], BF16)
    woutT = bump("woutT", [128, 4 * D], BF16)
    STAGE0 = ptr[0]
    kfm = bump("kfm", [128, 4 * S], BF16)
    qfm = bump("qfm", [128, 4 * 2048], BF16)
    vtm = bump("vtm", [128, NT * DH], BF16)
    P01_BASE = ptr[0]
    KB = 1024

    def at_off(name, off_kb, shape, dtype):
        return nc.alloc_sbuf_tensor_at(name, list(shape), dtype,
                                       offset=BASE0 + P01_BASE + int(off_kb * KB)).ap()

    xstream = at_off("xstream", 0, [128, 2 * D], F32)
    sqdump = at_off("sqdump", 8, [128, D], F32)
    xnbf = at_off("xnbf", 8, [128, 2 * D], BF16)
    xnfm = at_off("xnfm", 12, [128, 8 * 4 * 128], BF16)
    brow2 = at_off("brow2", 12, [1, 2 * D], F32)
    xcload = at_off("xcload", 0, [128, D], F32)
    xn32 = at_off("xn32", 4, [128, D], F32)
    xqbf = at_off("xqbf", 8, [128, D], BF16)
    xqst = at_off("xqst", 10, [128, 2 * 8 * 128], BF16)
    expP = at_off("expP", 14, [128, 2 * 512], BF16)
    avun = at_off("avun", 16, [128, 4 * 512], BF16)
    wraw = at_off("wraw", 20, [128, 2 * 1024], F32)
    wqbf = at_off("wqbf", 28, [128, 2 * 1024], BF16)
    wtst = at_off("wtst", 32, [128, 1024], BF16)
    vfmb = at_off("vfmb", 34, [128, 4 * 256], BF16)
    A_END = ptr[0] + 36 * KB
    ptr[0] = A_END
    P2_END = A_END

    ptr[0] = PERSIST_END
    h32 = bump("h32", [128, 32 * 512], F32)
    hq = bump("hq", [128, 32 * 512], BF16)
    xqfm = bump("xqfm", [128, 8 * 2048], BF16)
    wfc_l = bump("wfc_l", [128, 2 * 8 * 128], BF16)
    wpq_l = bump("wpq_l", [128, 2 * 1024], BF16)
    hqi = bump("hqi", [128, 2 * 512], I32)
    out_sb = bump("out_sb", [128, 2 * D], F32)
    B_END = ptr[0]

    PREP_END = A_END

    BASE = 16576
    LIMIT = nc.SBUF_PARTITION_SIZE_BYTES
    assert max(A_END, P2_END, B_END, PREP_END) + BASE <= LIMIT - 32, (
        A_END, P2_END, B_END, PREP_END, LIMIT)

    psb = [nc.alloc_psum_tensor(f"psb{k}", [128, 512], F32).ap() for k in range(8)]

    p = Prog()
    SD, SG, SC2, SV, SS, ST, SOUT, SP = "sD", "sG", "sC", "sV", "sS", "sT", "sO", "sP"

    def vop(fn, wait=()):
        p.op("vector", fn, wait=wait, inc=(SV, 1))

    def sop(fn, wait=()):
        p.op("scalar", fn, wait=wait, inc=(SS, 1))

    def top(fn, wait=()):
        p.op("tensor", fn, wait=wait, inc=(ST, 1))

    pa_v = {0: 0, 1: 0}
    pa_d = {0: 0, 1: 0}
    for j in range(12):
        sl = j % 2
        p.dma("sync", SD, wraw[:, sl * 1024:(sl + 1) * 1024],
              wqkv_d[j * 128:(j + 1) * 128, :], wait=[(SV, pa_v[sl])])
        dld = p.v(SD)
        vop(lambda e, s=sl: e.tensor_copy(out=wqbf[:, s * 1024:(s + 1) * 1024],
                                          in_=wraw[:, s * 1024:(s + 1) * 1024]),
            wait=[(SD, dld), (SD, pa_d[sl])])
        pa_v[sl] = p.v(SV)
        p.dma("sync", SD,
              wqkvT.rearrange("q (d c) -> q d c", c=1536)[:, :, j * 128:(j + 1) * 128],
              wqbf[:, sl * 1024:(sl + 1) * 1024].rearrange("q (d c) -> q d c", c=128),
              wait=[(SV, pa_v[sl])], transpose=True)
        pa_d[sl] = p.v(SD)
    wq_tr = p.v(SD)
    for jj in range(4):
        sl = jj % 2
        p.dma("sync", SD, wraw[:, sl * 1024:(sl + 1) * 1024].rearrange("q (a c) -> q a c", c=512),
              wout_d.rearrange("(j q) c -> q j c", q=128)[:, 2 * jj:2 * jj + 2, :],
              wait=[(SV, pa_v[sl])])
        dld = p.v(SD)
        vop(lambda e, s=sl: e.tensor_copy(out=wqbf[:, s * 1024:(s + 1) * 1024],
                                          in_=wraw[:, s * 1024:(s + 1) * 1024]),
            wait=[(SD, dld), (SD, pa_d[sl])])
        pa_v[sl] = p.v(SV)
        for k2 in range(2):
            j = 2 * jj + k2
            p.dma("sync", SD,
                  woutT.rearrange("q (d c) -> q d c", c=1024)[:, :, j * 128:(j + 1) * 128],
                  wqbf[:, sl * 1024 + k2 * 512:sl * 1024 + (k2 + 1) * 512].rearrange("q (d c) -> q d c", c=128),
                  wait=[(SV, pa_v[sl])] if k2 == 0 else [], transpose=True)
        pa_d[sl] = p.v(SD)
    wo_tr = p.v(SD)
    p.dma("sync", SD, brow2[0:1, 0:D], bout_d)
    p.dma("sync", SD, brow2[0:1, D:2 * D], bpj_d)
    p.dma("sync", SD, m0_t, m0_d)
    p.dma("sync", SD, m1_t, m1_d)
    p.dma("sync", SD, onc, onc_d)
    p.dma("sync", SD, onr, onr_d)
    p.dma("sync", SD, o11, o11_d)
    p.dma("sync", SD, kon, kon_d)
    p.dma("sync", SD, bfc_cols, bfc_d.rearrange("o (q t) -> (o q) t", q=128))
    c_ld = p.v(SD)
    vop(lambda e: e.tensor_tensor(out=brow2[0:1, 0:D], in0=brow2[0:1, 0:D],
                                  in1=brow2[0:1, D:2 * D], op=ALU.add), wait=[(SD, c_ld)])
    bs_v = p.v(SV)
    for h in range(2):
        top(lambda e, hh=h: e.matmul(psb[6 + hh], onr, brow2[0:1, hh * 512:(hh + 1) * 512],
                                     start=True, stop=True), wait=[(SV, bs_v)])
    bs_t = p.v(ST)
    for h in range(2):
        vop(lambda e, hh=h: e.tensor_copy(out=bsum_bc[:, hh * 512:(hh + 1) * 512],
                                          in_=psb[6 + hh]), wait=[(ST, bs_t)] if h == 0 else [])
    PREPA = {"v": p.v(SV), "d": p.v(SD), "t": p.v(ST), "s": p.v(SS)}

    chain_g = [0]

    def sw():
        return [(SV, p.v(SV))]

    def scale_chain(nb):
        sop(lambda e, n=nb: e.activation(out=c_t[:, 0:n], in_=c_am[:, 0:n], func=AF.Ln,
                                         bias=K_TINY), wait=[(SV, p.v(SV))])
        sv = p.v(SS)
        vop(lambda e, n=nb: e.tensor_scalar(out=c_t[:, 0:n], in0=c_t[:, 0:n],
                                            scalar1=1.0 / LN2, scalar2=-3.0,
                                            op0=ALU.mult, op1=ALU.add), wait=[(SS, sv)])
        vop(lambda e, n=nb: e.tensor_copy(out=c_i[:, 0:n], in_=c_t[:, 0:n]), wait=sw())
        vop(lambda e, n=nb: e.tensor_copy(out=c_f[:, 0:n], in_=c_i[:, 0:n]), wait=sw())
        vop(lambda e, n=nb: e.tensor_tensor(out=c_am[:, 0:n], in0=c_f[:, 0:n],
                                            in1=c_t[:, 0:n], op=ALU.is_gt), wait=sw())
        vop(lambda e, n=nb: e.tensor_tensor(out=c_f[:, 0:n], in0=c_f[:, 0:n],
                                            in1=c_am[:, 0:n], op=ALU.subtract), wait=sw())
        vv = p.v(SV)
        sop(lambda e, n=nb: e.activation(out=c_s[:, 0:n], in_=c_f[:, 0:n], func=AF.Exp,
                                         scale=K_LN2), wait=[(SV, vv)])
        sop(lambda e, n=nb: e.activation(out=c_si[:, 0:n], in_=c_f[:, 0:n], func=AF.Exp,
                                         scale=K_NLN2))
        sv = p.v(SS)
        vop(lambda e, n=nb: e.tensor_copy(out=c_bf[:, 0:n], in_=c_s[:, 0:n]), wait=[(SS, sv)])
        vop(lambda e, n=nb: e.tensor_copy(out=c_s[:, 0:n], in_=c_bf[:, 0:n]), wait=sw())
        vop(lambda e, n=nb: e.tensor_copy(out=c_bf[:, 0:n], in_=c_si[:, 0:n]), wait=sw())
        vop(lambda e, n=nb: e.tensor_copy(out=c_si[:, 0:n], in_=c_bf[:, 0:n]), wait=sw())
        chain_g[0] = p.v(SS)

    def quant_piece(raw_sl, out_sl):
        scale_chain(32)
        for hf in range(2):
            vop(lambda e, HF=hf, s=raw_sl: e.tensor_tensor(
                out=c_i[:, 0:512].rearrange("q (b w) -> q b w", w=32),
                in0=wraw[:, s * 1024 + HF * 512:s * 1024 + (HF + 1) * 512].rearrange("q (b w) -> q b w", w=32),
                in1=c_si[:, HF * 16:(HF + 1) * 16].unsqueeze(2).broadcast_to([128, 16, 32]),
                op=ALU.mult), wait=sw())
            vop(lambda e, HF=hf, s=out_sl: e.tensor_tensor(
                out=wqbf[:, s * 1024 + HF * 512:s * 1024 + (HF + 1) * 512].rearrange("q (b w) -> q b w", w=32),
                in0=c_i[:, 0:512].rearrange("q (b w) -> q b w", w=32),
                in1=c_s[:, HF * 16:(HF + 1) * 16].unsqueeze(2).broadcast_to([128, 16, 32]),
                op=ALU.mult), wait=sw())

    def prep_b_gen():
        g_raw = {0: 0, 1: 0}
        g_bf_sg = {0: 0, 1: 0}
        g_bf_sd = {0: 0, 1: 0}
        g_tr = 0
        first = True

        def piece_src(idx):
            if idx < 32:
                return wfc_d[idx * 128:(idx + 1) * 128, :]
            j, q4 = divmod(idx - 32, 4)
            return wpj_d[j * 128:(j + 1) * 128, q4 * 1024:(q4 + 1) * 1024]

        def ld(idx, grp, wait):
            eng, sem = ("gpsimd", SG) if grp < 2 else ("sync", SD)
            p.dma(eng, sem, wraw[:, (idx % 2) * 1024:(idx % 2 + 1) * 1024],
                  piece_src(idx), wait=wait)
            return (sem, p.v(sem))

        for grp in range(4):
            base = grp * 16
            lds = {}
            w0 = [(SV, PREPA["v"]), (SD, PREPA["d"])] if first else [(SV, g_raw[base % 2])]
            first = False
            lds[base] = ld(base, grp, w0)
            for jj in range(16):
                idx = base + jj
                sl = jj % 2
                if jj + 1 < 16:
                    lds[idx + 1] = ld(idx + 1, grp, [(SV, g_raw[(idx + 1) % 2])])
                vop(lambda e, s=sl, JJ=jj: e.tensor_reduce(
                    out=c_am[:, 32 * JJ:32 * JJ + 32],
                    in_=wraw[:, s * 1024:(s + 1) * 1024].rearrange("q (b w) -> q b w", w=32),
                    axis=AX.X, op=ALU.max, apply_absolute_value=True),
                    wait=[lds[idx]] + ([(SS, chain_g[0])] if jj == 0 else []))
                g_raw[sl] = p.v(SV)
                yield
            scale_chain(512)
            yield
            lds[base] = ld(base, grp, [(SV, g_raw[base % 2])])
            for jj in range(16):
                idx = base + jj
                sl = jj % 2
                if jj + 1 < 16:
                    lds[idx + 1] = ld(idx + 1, grp, [(SV, g_raw[(idx + 1) % 2])])
                for hf in range(2):
                    vop(lambda e, s=sl, HF=hf, JJ=jj: e.tensor_tensor(
                        out=c_i[:, 0:512].rearrange("q (b w) -> q b w", w=32),
                        in0=wraw[:, s * 1024 + HF * 512:s * 1024 + (HF + 1) * 512].rearrange("q (b w) -> q b w", w=32),
                        in1=c_si[:, 32 * JJ + HF * 16:32 * JJ + (HF + 1) * 16].unsqueeze(2).broadcast_to([128, 16, 32]),
                        op=ALU.mult),
                        wait=([lds[idx]] if hf == 0 else []) + sw() +
                             ([(SG, g_bf_sg[sl]), (SD, g_bf_sd[sl])] if hf == 0 else []))
                    vop(lambda e, s=sl, HF=hf, JJ=jj: e.tensor_tensor(
                        out=wqbf[:, s * 1024 + HF * 512:s * 1024 + (HF + 1) * 512].rearrange("q (b w) -> q b w", w=32),
                        in0=c_i[:, 0:512].rearrange("q (b w) -> q b w", w=32),
                        in1=c_s[:, 32 * JJ + HF * 16:32 * JJ + (HF + 1) * 16].unsqueeze(2).broadcast_to([128, 16, 32]),
                        op=ALU.mult), wait=sw())
                g_raw[sl] = p.v(SV)
                qv = p.v(SV)
                if idx < 32:
                    p.dma("gpsimd", SG, wfcq_s[idx * 128:(idx + 1) * 128, :],
                          wqbf[:, sl * 1024:(sl + 1) * 1024], wait=[(SV, qv)])
                    g_bf_sg[sl] = p.v(SG)
                else:
                    j, q4 = divmod(idx - 32, 4)
                    p.dma("sync", SD, wtst.rearrange("q (b c) -> q b c", c=128),
                          wqbf[:, sl * 1024:(sl + 1) * 1024].rearrange("q (b c) -> q b c", c=128),
                          wait=[(SV, qv), (SD, g_tr)], transpose=True)
                    g_bf_sd[sl] = p.v(SD)
                    p.dma("sync", SD,
                          wpqT_s.rearrange("(b q) c -> q b c", q=128)[:, 8 * q4:8 * q4 + 8, j * 128:(j + 1) * 128],
                          wtst.rearrange("q (b c) -> q b c", c=128),
                          wait=[(SD, g_bf_sd[sl])])
                    g_tr = p.v(SD)
                yield

    PB = prep_b_gen()
    pb_cnt = [0]

    def pb(n=1):
        for _ in range(n):
            if next(PB, -1) != -1:
                pb_cnt[0] += 1

    def pb_align():
        while pb_cnt[0] % 33 != 0:
            pb(1)

    xsA = {}
    STATS_B = {}
    for t in range(NT):
        sl = t % 2
        w = [(SV, xsA.get(("v", sl), 0)), (SS, xsA.get(("s", sl), 0))]
        p.dma("sync", SD, xstream[:, sl * D:(sl + 1) * D],
              xf_d[t * 128:(t + 1) * 128, :], wait=w)
        xld = p.v(SD)
        vop(lambda e, s=sl, tt=t: e.reduce_sum(
            out=ssum[:, tt:tt + 1], in_=xstream[:, s * D:(s + 1) * D], axis=AX.X),
            wait=[(SD, xld)])
        xsA[("v", sl)] = p.v(SV)
        sop(lambda e, s=sl, tt=t: e.activation(
            out=sqdump, in_=xstream[:, s * D:(s + 1) * D], func=AF.Square,
            accum_out=ssq[:, tt:tt + 1]), wait=[(SD, xld)])
        sop(lambda e: e.activation(out=sdum, in_=kon[:, 7:8], func=AF.Square))
        xsA[("s", sl)] = p.v(SS)
        pb(1)
        if t % 8 == 7:
            b0, b1 = t - 7, t + 1
            sv = p.v(SS)
            vop(lambda e, a=b0, b=b1: e.tensor_scalar(
                out=st1[:, a:b], in0=ssq[:, a:b], scalar1=1.0 / D,
                scalar2=None, op0=ALU.mult), wait=[(SS, sv)] + sw())
            vop(lambda e, a=b0, b=b1: e.tensor_scalar(
                out=negm[:, a:b], in0=ssum[:, a:b], scalar1=-1.0 / D,
                scalar2=None, op0=ALU.mult), wait=sw())
            vop(lambda e, a=b0, b=b1: e.tensor_tensor(
                out=st2[:, a:b], in0=negm[:, a:b], in1=negm[:, a:b], op=ALU.mult),
                wait=sw())
            vop(lambda e, a=b0, b=b1: e.tensor_tensor(
                out=st1[:, a:b], in0=st1[:, a:b], in1=st2[:, a:b], op=ALU.subtract),
                wait=sw())
            vv = p.v(SV)
            sop(lambda e, a=b0, b=b1: e.activation(
                out=st2[:, a:b], in_=st1[:, a:b], func=AF.Ln, bias=K_EPS),
                wait=[(SV, vv)])
            sop(lambda e, a=b0, b=b1: e.activation(
                out=rstd[:, a:b], in_=st2[:, a:b], func=AF.Exp, scale=K_NHALF))
            STATS_B[t // 8] = p.v(SS)
    STATS_S = p.v(SS)

    xsB = {}
    xnbf_g = {0: 0, 1: 0}
    xnfm_g = {0: 0, 1: 0, 2: 0, 3: 0}
    bank_g = {}
    vtr_g = [0]
    p0_tr = {}

    def emit_pB_chunk(c):
        for k in range(2):
            t = 2 * c + k
            sl = t % 2
            w = [(SV, xsB.get(("v", sl), xsA.get(("v", sl), 0)))]
            if c < 2:
                w += [(SS, STATS_B[0])]
            p.dma("sync", SD, xstream[:, sl * D:(sl + 1) * D],
                  xf_d[t * 128:(t + 1) * 128, :], wait=w)
            xld = p.v(SD)
            wslot = t % 4
            vop(lambda e, s=sl, tt=t: e.tensor_scalar(
                out=xnbf[:, s * D:(s + 1) * D], in0=xstream[:, s * D:(s + 1) * D],
                scalar1=negm[:, tt:tt + 1], scalar2=rstd[:, tt:tt + 1],
                op0=ALU.add, op1=ALU.mult),
                wait=[(SD, xld), (SS, STATS_B[t // 8]), (SD, xnbf_g[sl])])
            xn_v = p.v(SV)
            wtr = [(SV, xn_v), (ST, xnfm_g[wslot])]
            if c == 0 and k == 0:
                wtr.append((ST, PREPA["t"]))
            p.dma("scalar", SC2,
                  xnfm.rearrange("q (d c) -> q d c", c=512)[:, :, wslot * 128:(wslot + 1) * 128],
                  xnbf[:, sl * D:(sl + 1) * D].rearrange("q (d c) -> q d c", c=128),
                  wait=wtr, transpose=True)
            xnbf_g[sl] = p.v(SC2)
            xsB[("v", sl)] = p.v(SV)
        p0_tr[c] = p.v(SC2)

    def emit_p1_chunk(c):
        s0 = (2 * c) % 4
        tr_done = p0_tr[c]
        for ec in range(4):
            bk = ec % 2
            for dmc in range(8):
                rr = xnfm[:, dmc * 512 + s0 * 128:dmc * 512 + s0 * 128 + 256]
                ll = wqkvT[:, dmc * 1536 + 512 + ec * 128:dmc * 1536 + 640 + ec * 128]
                w = [(SC2, tr_done), (SV, bank_g.get(bk, 0))] if dmc == 0 else []
                if dmc == 0 and ec == 0 and c == 0:
                    w.append((SD, wq_tr))
                top(lambda e, L=ll, R=rr, B=bk, M=dmc: e.matmul(
                    psb[B][:, 0:256], L, R, start=(M == 0), stop=(M == 7)), wait=w)
            mm = p.v(ST)
            vop(lambda e, E=ec, B=bk, C=c: e.tensor_copy(
                out=kfm[:, E * S + C * 256:E * S + (C + 1) * 256], in_=psb[B][:, 0:256]),
                wait=[(ST, mm)])
            bank_g[bk] = p.v(SV)
        for ec in range(4):
            bk = ec % 2
            for dmc in range(8):
                rr = xnfm[:, dmc * 512 + s0 * 128:dmc * 512 + s0 * 128 + 256]
                ll = wqkvT[:, dmc * 1536 + 1024 + ec * 128:dmc * 1536 + 1152 + ec * 128]
                w = [(SV, bank_g.get(bk, 0))] if dmc == 0 else []
                if dmc == 0 and ec == 0:
                    w.append((SC2, vtr_g[0]))
                top(lambda e, L=ll, R=rr, B=bk, M=dmc: e.matmul(
                    psb[B][:, 0:256], L, R, start=(M == 0), stop=(M == 7)), wait=w)
            mm = p.v(ST)
            vop(lambda e, E=ec, B=bk: e.tensor_copy(
                out=vfmb[:, E * 256:(E + 1) * 256], in_=psb[B][:, 0:256]), wait=[(ST, mm)])
            bank_g[bk] = p.v(SV)
        vv2 = p.v(SV)
        for ec in range(4):
            p.dma("scalar", SC2,
                  vtm.rearrange("q (a c) -> q a c", c=512)[:, 2 * c:2 * c + 2, ec * 128:(ec + 1) * 128],
                  vfmb[:, ec * 256:(ec + 1) * 256].rearrange("q (a c) -> q a c", c=128),
                  wait=[(SV, vv2)] if ec == 0 else [], transpose=True)
        vtr_g[0] = p.v(SC2)
        for ec in range(4):
            bk = 2 + ec % 2
            for dmc in range(8):
                rr = xnfm[:, dmc * 512 + s0 * 128:dmc * 512 + s0 * 128 + 128]
                ll = wqkvT[:, dmc * 1536 + ec * 128:dmc * 1536 + (ec + 1) * 128]
                w = [(SV, bank_g.get(bk, 0))] if dmc == 0 else []
                top(lambda e, L=ll, R=rr, B=bk, M=dmc: e.matmul(
                    psb[B][:, 0:128], L, R, start=(M == 0), stop=(M == 7)), wait=w)
            mm = p.v(ST)
            vop(lambda e, E=ec, B=bk, C=c: e.tensor_copy(
                out=qfm[:, E * 2048 + C * 128:E * 2048 + (C + 1) * 128], in_=psb[B][:, 0:128]),
                wait=[(ST, mm)])
            bank_g[bk] = p.v(SV)
        xnfm_g[s0] = p.v(ST)
        xnfm_g[(s0 + 1) % 4] = p.v(ST)

    for c in range(NCH):
        if c >= 2:
            emit_p1_chunk(c - 2)
        emit_pB_chunk(c)
        pb(2)
    emit_p1_chunk(NCH - 2)
    emit_p1_chunk(NCH - 1)
    P1 = {"v": p.v(SV), "d": p.v(SD), "t": p.v(ST), "g": p.v(SG), "c": p.v(SC2)}

    pc_state = {}

    def pass_c_gen():
        xc_g = 0
        xq_g = {}
        for i in range(NOWN):
            t = 2 * i
            w = [(SV, xc_g)]
            if i == 0:
                w += [(SV, P1["v"]), (ST, P1["t"]), (SD, P1["d"])]
            p.dma("sync", SD, xcload, xo_d[i * 128:(i + 1) * 128, :], wait=w)
            xld = p.v(SD)
            vop(lambda e, tt=t: e.tensor_scalar(
                out=xn32, in0=xcload,
                scalar1=negm[:, tt:tt + 1], scalar2=rstd[:, tt:tt + 1],
                op0=ALU.add, op1=ALU.mult), wait=[(SD, xld)] + sw())
            xc_g = p.v(SV)
            vop(lambda e, I=i: e.tensor_reduce(
                out=c_am[:, 32 * I:32 * I + 32],
                in_=xn32.rearrange("q (b w) -> q b w", w=32),
                axis=AX.X, op=ALU.max, apply_absolute_value=True),
                wait=([(SS, chain_g[0])] if i == 0 else []))
            yield
        scale_chain(512)
        yield
        for i in range(NOWN):
            t = 2 * i
            ssl = i % 2
            p.dma("sync", SD, xcload, xo_d[i * 128:(i + 1) * 128, :], wait=[(SV, xc_g)])
            xld = p.v(SD)
            vop(lambda e, tt=t: e.tensor_scalar(
                out=xn32, in0=xcload,
                scalar1=negm[:, tt:tt + 1], scalar2=rstd[:, tt:tt + 1],
                op0=ALU.add, op1=ALU.mult), wait=[(SD, xld)] + sw())
            xc_g = p.v(SV)
            for hf in range(2):
                vop(lambda e, HF=hf, I=i: e.tensor_tensor(
                    out=c_i[:, 0:512].rearrange("q (b w) -> q b w", w=32),
                    in0=xn32[:, HF * 512:(HF + 1) * 512].rearrange("q (b w) -> q b w", w=32),
                    in1=c_si[:, 32 * I + HF * 16:32 * I + (HF + 1) * 16].unsqueeze(2).broadcast_to([128, 16, 32]),
                    op=ALU.mult), wait=sw())
                vop(lambda e, HF=hf, I=i: e.tensor_tensor(
                    out=xqbf[:, HF * 512:(HF + 1) * 512].rearrange("q (b w) -> q b w", w=32),
                    in0=c_i[:, 0:512].rearrange("q (b w) -> q b w", w=32),
                    in1=c_s[:, 32 * I + HF * 16:32 * I + (HF + 1) * 16].unsqueeze(2).broadcast_to([128, 16, 32]),
                    op=ALU.mult),
                    wait=[(SD, xq_g.get(("tr", 1 - ssl), 0))] if hf == 0 else [])
            xq_v = p.v(SV)
            p.dma("sync", SD,
                  xqst[:, ssl * 1024:(ssl + 1) * 1024].rearrange("q (d c) -> q d c", c=128),
                  xqbf.rearrange("q (d c) -> q d c", c=128),
                  wait=[(SV, xq_v), (SG, xq_g.get(("sp", ssl), 0))], transpose=True)
            xq_g[("tr", ssl)] = p.v(SD)
            p.dma("gpsimd", SG,
                  xq_s.rearrange("q (d c) -> q d c", c=2048)[:, :, i * 128:(i + 1) * 128],
                  xqst[:, ssl * 1024:(ssl + 1) * 1024].rearrange("q (d c) -> q d c", c=128),
                  wait=[(SD, xq_g[("tr", ssl)])])
            xq_g[("sp", ssl)] = p.v(SG)
            yield
        pc_state["done"] = True

    PC = pass_c_gen()
    pc_state["done"] = False

    pb_align()
    sc_rd = {4: 0, 5: 0}
    expP_rd_v = {0: 0, 1: 0}
    expP_rd_t = {0: 0, 1: 0}
    av_prev_ev = 0
    prev_acc_v = 0
    denom_rd_t = 0
    rc_v = 0
    xpb_v_free = 0

    for g in range(4):
        ktmax = 8 * g + 8
        gsl = g % 2

        def emit_scores(kt, g=g, gsl=gsl):
            cstart = max(0, kt // 2 - 4 * g) * 128
            cols = 512 - cstart
            qcol0 = g * 512 + cstart
            sb = 4 + (kt % 2)
            for dc in range(4):
                w = []
                if dc == 0:
                    w = [(SS, sc_rd[sb])]
                    if kt == 0 and g == 0:
                        w += [(SV, P1["v"]), (SD, P1["d"]), (SC2, P1["c"])]
                ll = kfm[:, dc * S + kt * 128:dc * S + (kt + 1) * 128]
                rr = qfm[:, dc * 2048 + qcol0:dc * 2048 + qcol0 + cols]
                top(lambda e, L=ll, R=rr, B=sb, DC=dc, CO=cols: e.matmul(
                    psb[B][:, 0:CO], L, R, start=(DC == 0), stop=(DC == 3)), wait=w)
            return p.v(ST), cols, cstart

        sc_done, cols, cstart = emit_scores(0)
        for kt in range(ktmax):
            esl = kt % 2
            sb = 4 + esl
            sop(lambda e, B=sb, CO=cols, E=esl: e.activation(
                out=expP[:, E * 512:E * 512 + CO], in_=psb[B][:, 0:CO],
                func=AF.Exp, bias=K_NCSH, scale=K_ASC),
                wait=[(ST, sc_done), (SP, expP_rd_v[esl]), (ST, expP_rd_t[esl])])
            sc_rd[sb] = p.v(SS)
            exp_done = p.v(SS)
            wv = [(SS, exp_done)]
            if kt >= 8 * g:
                mt = m0_t if kt % 2 == 0 else m1_t
                p.op("gpsimd", lambda e, E=esl, M=mt: e.tensor_tensor(
                    out=expP[:, E * 512:E * 512 + 128], in0=expP[:, E * 512:E * 512 + 128],
                    in1=M, op=ALU.mult), wait=[(SS, exp_done)], inc=(SP, 1))
                wv.append((SP, p.v(SP)))
            expP_rd_v[esl] = p.v(SP)
            cs_cur, co_cur = cstart, cols
            if kt + 1 < ktmax:
                sc_done, cols, cstart = emit_scores(kt + 1)
            for dc in range(4):
                ll = vtm[:, kt * 512 + dc * 128:kt * 512 + (dc + 1) * 128]
                w = list(wv) if dc == 0 else []
                if dc == 0 and kt == 0:
                    w += [(SV, prev_acc_v), (SV, av_prev_ev)]
                top(lambda e, L=ll, DC=dc, CS=cs_cur, CO=co_cur, E=esl, F=(kt == 0),
                    LAST=(kt == ktmax - 1): e.matmul(
                    psb[DC][:, CS:CS + CO], L, expP[:, E * 512:E * 512 + CO],
                    start=F, stop=LAST, skip_group_check=True), wait=w)
            top(lambda e, CS=cs_cur, CO=co_cur, E=esl, F=(kt == 0),
                LAST=(kt == ktmax - 1): e.matmul(
                psb[6][0:1, CS:CS + CO], onc, expP[:, E * 512:E * 512 + CO],
                start=F, stop=LAST, skip_group_check=True))
            expP_rd_t[esl] = p.v(ST)
            if not pc_state["done"]:
                next(PC, None)
            else:
                pb(2)
        av_mm = p.v(ST)
        vop(lambda e: e.tensor_copy(out=denom_r, in_=psb[6][0:1, 0:512]),
            wait=[(ST, av_mm), (ST, denom_rd_t)])
        for dc in range(4):
            w = [(ST, av_mm)] if dc == 0 else []
            vop(lambda e, DC=dc: e.tensor_copy(
                out=avun[:, DC * 512:(DC + 1) * 512], in_=psb[DC]), wait=w)
        av_prev_ev = p.v(SV)
        dn_v = p.v(SV)
        for ti in range(4):
            i = 4 * g + ti
            top(lambda e, TI=ti: e.matmul(psb[7][:, 0:1],
                                          denom_r[0:1, TI * 128:(TI + 1) * 128], o11,
                                          start=True, stop=True),
                wait=[(SV, dn_v), (SV, rc_v)])
            vop(lambda e, I=i: e.reciprocal(out=recips[:, I:I + 1], in_=psb[7][:, 0:1]),
                wait=[(ST, p.v(ST))])
            rc_v = p.v(SV)
            p.dma("sync", SD, xpb, xo_d[i * 128:(i + 1) * 128, :], wait=[(SV, xpb_v_free)])
            vop(lambda e: e.tensor_tensor(out=xpb, in0=xpb, in1=bsum_bc, op=ALU.add),
                wait=[(SD, p.v(SD))])
            xpb_v = p.v(SV)
            for dc in range(4):
                for oh in range(2):
                    w = []
                    if dc == 0 and oh == 0:
                        w = [(SV, av_prev_ev), (SV, prev_acc_v)]
                        if g == 0 and ti == 0:
                            w.append((SD, wo_tr))
                    ll = avun[:, dc * 512 + ti * 128:dc * 512 + (ti + 1) * 128]
                    rr2 = woutT[:, dc * D + oh * 512:dc * D + (oh + 1) * 512]
                    top(lambda e, L=ll, R=rr2, OH=oh, DC=dc: e.matmul(
                        psb[OH], L, R, start=(DC == 0), stop=(DC == 3)), wait=w)
            opmm = p.v(ST)
            for oh in range(2):
                vop(lambda e, OH=oh, I=i: e.scalar_tensor_tensor(
                    out=acc[:, I * D + OH * 512:I * D + (OH + 1) * 512],
                    in0=psb[OH], scalar=recips[:, I:I + 1],
                    in1=xpb[:, OH * 512:(OH + 1) * 512],
                    op0=ALU.mult, op1=ALU.add),
                    wait=([(ST, opmm), (SV, xpb_v)] + sw()) if oh == 0 else [])
            prev_acc_v = p.v(SV)
            xpb_v_free = p.v(SV)
        denom_rd_t = p.v(ST)
    for _ in PC:
        pass
    for _ in PB:
        pass
    P2 = {"v": p.v(SV), "s": p.v(SS), "t": p.v(ST), "d": p.v(SD), "g": p.v(SG)}
    if dbg:
        p.dma("gpsimd", SG, dbg_acc, acc, wait=[(SV, P2["v"])])
        p.dma("gpsimd", SG, dbg_rec, recips)

    p.dma("sync", SD, xqfm, xq_s, wait=[(SV, P2["v"]), (ST, P2["t"]), (SS, P2["s"]),
                                        (SG, P2["g"])])
    xq_ld = p.v(SD)
    wfc_rows = wfcq_s.rearrange("(q s) c -> s q c", s=32)
    wpq_rows = wpqT_s.rearrange("(q s) c -> s q c", s=32)
    wfc_g = {0: 0, 1: 0}
    wpq_g = {0: 0, 1: 0}
    fc_rd = {0: 0, 1: 0}
    pj_guard = 0
    h32_rd_prev = 0
    hq_rd_prev_t = 0
    out_sb_g = {0: 0, 1: 0}
    for sp in range(4):
        for hc in range(32):
            sl = hc % 2
            w = [(ST, wfc_g[sl])]
            if sp == 0 and hc == 0:
                w += [(SD, xq_ld), (ST, P2["t"]), (SG, P2["g"])]
            p.dma("sync", SD,
                  wfc_l[:, sl * 1024:(sl + 1) * 1024].rearrange("q (b c) -> q b c", c=128),
                  wfc_rows[hc].rearrange("q (b c) -> q b c", c=128),
                  wait=w, transpose=True)
            trd = p.v(SD)
            bk = hc % 2
            for dmc in range(8):
                w = []
                if dmc == 0:
                    w = [(SD, trd), (SS, fc_rd[bk])]
                    if sp == 0 and hc < 2:
                        w += [(SV, P2["v"])]
                ll = wfc_l[:, sl * 1024 + dmc * 128:sl * 1024 + (dmc + 1) * 128]
                rr = xqfm[:, dmc * 2048 + sp * 512:dmc * 2048 + (sp + 1) * 512]
                top(lambda e, L=ll, R=rr, B=bk, M=dmc: e.matmul(
                    psb[B], L, R, start=(M == 0), stop=(M == 7)), wait=w)
            wfc_g[sl] = p.v(ST)
            mmv = p.v(ST)
            w = [(ST, mmv)]
            if hc == 0:
                w.append((SV, h32_rd_prev))
            sop(lambda e, HC=hc, B=bk: e.activation(
                out=h32[:, HC * 512:(HC + 1) * 512], in_=psb[B],
                func=AF.Gelu, bias=bfc_cols[:, HC:HC + 1]), wait=w)
            fc_rd[bk] = p.v(SS)
            w = [(SS, fc_rd[bk])]
            if hc == 0:
                w.append((SS, chain_g[0]))
            vop(lambda e, HC=hc, FIRST=(hc == 0): e.tensor_tensor(
                out=c_am[:, 0:512],
                in0=(h32[:, HC * 512:(HC + 1) * 512] if FIRST else c_am[:, 0:512]),
                in1=h32[:, HC * 512:(HC + 1) * 512], op=ALU.max), wait=w)
            vop(lambda e, HC=hc, FIRST=(hc == 0): e.tensor_tensor(
                out=c_t[:, 0:512],
                in0=(h32[:, HC * 512:(HC + 1) * 512] if FIRST else c_t[:, 0:512]),
                in1=h32[:, HC * 512:(HC + 1) * 512], op=ALU.min))
        vop(lambda e: e.tensor_scalar(out=c_t[:, 0:512], in0=c_t[:, 0:512],
                                      scalar1=-1.0, scalar2=None, op0=ALU.mult),
            wait=sw())
        vop(lambda e: e.tensor_tensor(out=c_am[:, 0:512], in0=c_am[:, 0:512],
                                      in1=c_t[:, 0:512], op=ALU.max), wait=sw())
        scale_chain(512)
        hq_done = {}
        for hc in range(32):
            qsl = hc % 2
            w = sw()
            if hc == 0:
                w += [(ST, hq_rd_prev_t)]
            vop(lambda e, HC=hc, QS=qsl: e.tensor_tensor(
                out=hqi[:, QS * 512:(QS + 1) * 512], in0=h32[:, HC * 512:(HC + 1) * 512],
                in1=c_si[:, 0:512], op=ALU.mult), wait=w)
            vop(lambda e, HC=hc, QS=qsl: e.tensor_tensor(
                out=hq[:, HC * 512:(HC + 1) * 512], in0=hqi[:, QS * 512:(QS + 1) * 512],
                in1=c_s[:, 0:512], op=ALU.mult))
            hq_done[hc] = p.v(SV)
        h32_rd_prev = p.v(SV)
        for pair in range(2):
            for hc in range(32):
                sl = hc % 2
                w = [(ST, wpq_g[sl])]
                if sp == 0 and pair == 0 and hc < 2:
                    w += [(ST, P2["t"]), (SG, P2["g"])]
                p.dma("scalar", SC2, wpq_l[:, sl * 1024:(sl + 1) * 1024], wpq_rows[hc],
                      wait=w)
                wld = p.v(SC2)
                for ti in range(2):
                    ilocal = pair * 2 + ti
                    for oh in range(2):
                        w = []
                        if ti == 0 and oh == 0:
                            w = [(SC2, wld), (SV, hq_done[hc if pair == 0 else 31])]
                            if hc == 0:
                                w += [(SV, pj_guard)]
                        ll = hq[:, hc * 512 + ilocal * 128:hc * 512 + (ilocal + 1) * 128]
                        rr2 = wpq_l[:, sl * 1024 + oh * 512:sl * 1024 + (oh + 1) * 512]
                        top(lambda e, L=ll, R=rr2, TI=ti, OH=oh, HC=hc: e.matmul(
                            psb[2 + TI * 2 + OH], L, R, start=(HC == 0), stop=(HC == 31)),
                            wait=w)
                wpq_g[sl] = p.v(ST)
            pjmm = p.v(ST)
            for ti in range(2):
                i = 4 * sp + pair * 2 + ti
                osl = ti
                for oh in range(2):
                    w = []
                    if oh == 0:
                        w = [(ST, pjmm), (SOUT, out_sb_g[osl])]
                    vop(lambda e, TI=ti, OH=oh, I=i, OS=osl: e.tensor_tensor(
                        out=out_sb[:, OS * D + OH * 512:OS * D + (OH + 1) * 512],
                        in0=psb[2 + TI * 2 + OH],
                        in1=acc[:, I * D + OH * 512:I * D + (OH + 1) * 512],
                        op=ALU.add), wait=w)
                ev = p.v(SV)
                p.dma("gpsimd", SOUT, y_d[i * 128:(i + 1) * 128, :],
                      out_sb[:, osl * D:(osl + 1) * D], wait=[(SV, ev)])
                out_sb_g[osl] = p.v(SOUT)
            pj_guard = p.v(SV)
        hq_rd_prev_t = p.v(ST)

    if dbg:
        p.dma("gpsimd", SG, dbg_h32, h32, wait=[(SV, p.v(SV)), (SS, p.v(SS))])
        p.dma("gpsimd", SG, dbg_hq, hq)
        p.op("gpsimd", lambda e: e.engine_nop(), wait=[(SG, p.v(SG))])
    p.op("gpsimd", lambda e: e.engine_nop(), wait=[(SOUT, p.v(SOUT))])

    from contextlib import ExitStack
    with ExitStack() as ctx:
        sems = {}
        for s in [SD, SG, SC2, SV, SS, ST, SOUT, SP]:
            sems[s] = ctx.enter_context(nc.semaphore(s))
        block = ctx.enter_context(nc.Block())

        @block.sync
        def _(eng):
            run_stream(p, "sync", eng, sems)

        @block.vector
        def _(eng):
            run_stream(p, "vector", eng, sems)

        @block.scalar
        def _(eng):
            run_stream(p, "scalar", eng, sems)

        @block.tensor
        def _(eng):
            run_stream(p, "tensor", eng, sems)

        @block.gpsimd
        def _(eng):
            run_stream(p, "gpsimd", eng, sems)
    return nc


_NC_CACHE = [None]


def kernel(L0_attn_x, L0_attn_w_qkv, L0_attn_w_out, L0_attn_b_out,
           L0_ffn_w_fc, L0_ffn_b_fc, L0_ffn_w_proj, L0_ffn_b_proj):
    x = np.ascontiguousarray(np.asarray(L0_attn_x, dtype=np.float32))
    wqkv = np.ascontiguousarray(np.asarray(L0_attn_w_qkv, dtype=np.float32))
    wout = np.ascontiguousarray(np.asarray(L0_attn_w_out, dtype=np.float32))
    bout = np.ascontiguousarray(np.asarray(L0_attn_b_out, dtype=np.float32)).reshape(1, D)
    wfc = np.ascontiguousarray(np.asarray(L0_ffn_w_fc, dtype=np.float32))
    bfc = np.ascontiguousarray(np.asarray(L0_ffn_b_fc, dtype=np.float32)).reshape(1, HID)
    wpj = np.ascontiguousarray(np.asarray(L0_ffn_w_proj, dtype=np.float32))
    bpj = np.ascontiguousarray(np.asarray(L0_ffn_b_proj, dtype=np.float32)).reshape(1, D)

    if _NC_CACHE[0] is None:
        _NC_CACHE[0] = build()
    nc = _NC_CACHE[0]

    ii, jj = np.meshgrid(np.arange(128), np.arange(128), indexing="ij")
    tril_keep = (ii <= jj).astype(ml_dtypes.bfloat16)
    zeros = np.zeros((128, 128), ml_dtypes.bfloat16)
    ones = np.ones((128, 128), ml_dtypes.bfloat16)
    ones_col = np.ones((128, 1), ml_dtypes.bfloat16)
    ones_row = np.ones((1, 128), np.float32)
    one11 = np.ones((1, 1), np.float32)
    konst = np.tile(np.array([[TINY, EPS, -CSH, LN2, -LN2, -0.5, ATT_SC, 0.0]],
                             np.float32), (128, 1))

    swap = np.arange(NT).reshape(-1, 2)[:, ::-1].reshape(-1)
    in_maps = []
    for c in range(8):
        b, r = c // 2, c % 2
        xl = x[b].reshape(NT, 128, D)
        if r:
            xl = xl[swap]
        in_maps.append({
            "x_full": np.ascontiguousarray(xl.reshape(S, D)),
            "x_own": np.ascontiguousarray(xl[0::2].reshape(NOWN * 128, D)),
            "wqkv": wqkv, "wout": wout, "bout": bout,
            "wfc": wfc, "bfc": bfc, "wproj": wpj, "bproj": bpj,
            "m0": np.ascontiguousarray(tril_keep),
            "m1": np.ascontiguousarray(ones if r else zeros),
            "ones_col": ones_col, "ones_row": ones_row, "one11": one11,
            "konst": konst,
        })
    res = run_bass_kernel_spmd(nc, in_maps, core_ids=list(range(8)))
    out = np.empty((4, S, D), np.float32)
    for c, rr in enumerate(res.results):
        b, r = c // 2, c % 2
        y = rr["y"].reshape(NOWN, 128, D)
        ob = out[b].reshape(NT, 128, D)
        for i in range(NOWN):
            ob[2 * i + r] = y[i]
    return out
